# version 12
# baseline (speedup 1.0000x reference)
import numpy as np
import ml_dtypes
import concourse.bass as bass
import concourse.mybir as mybir
from concourse.bass_utils import run_bass_kernel_spmd
from contextlib import ExitStack

BF = ml_dtypes.bfloat16
BF16 = mybir.dt.bfloat16
F32 = mybir.dt.float32
U8 = mybir.dt.uint8
AF = mybir.ActivationFunctionType
ALU = mybir.AluOpType

NCORES = 8
NTOK = 2048
TT = 1024
NTILES = NTOK // TT
D = 4096
KC = 32
HKV = 8
NT2 = TT // 512
EPS_LN = 1e-5
EPS_DEN = 1e-6

_cache = {}
META = []
DBG = False
SERIAL = 0


def _build_G():
    n = D
    j = np.arange(n)[None, :].astype(np.float64)
    G = np.zeros((n, n), dtype=np.float64)
    G[0, :] = 1.0 / np.sqrt(n)
    maxf = (n - 1) // 2
    f = np.arange(1, maxf + 1)[:, None].astype(np.float64)
    ang = 2.0 * np.pi * f * j / n
    G[1:1 + 2 * maxf:2, :] = np.sqrt(2.0 / n) * np.cos(ang)
    G[2:2 + 2 * maxf:2, :] = np.sqrt(2.0 / n) * np.sin(ang)
    G[n - 1, :] = ((-1.0) ** j[0]) / np.sqrt(n)
    return G.astype(BF)


ENG = ("t", "v", "a", "p")
DMAS = ("c", "m", "cf", "mk", "fo0", "fo1", "fo2", "fo3", "fw0", "fw1", "fw2", "fw3",
        "wf0", "wf1", "ga", "gb", "out", "x0", "x1", "x2", "x3", "wq0", "wq1",
        "wb0", "wb1", "db")
ALLSEMS = ENG + DMAS


def build_program():
    META.clear()
    nc = bass.Bass("TRN2", target_bir_lowering=False)

    xT = nc.dram_tensor("xT", [D, NTOK], F32, kind="ExternalInput")
    wqd = nc.dram_tensor("wqd", [HKV, 128, KC, 128], F32, kind="ExternalInput")
    bqd = nc.dram_tensor("bqd", [128, HKV], F32, kind="ExternalInput")
    mhd = nc.dram_tensor("mhd", [128, HKV, 128], F32, kind="ExternalInput")
    zmd = nc.dram_tensor("zmd", [128, HKV, HKV], F32, kind="ExternalInput")
    wod = nc.dram_tensor("wod", [D, D], F32, kind="ExternalInput")
    wbd = nc.dram_tensor("wbd", [KC, 128, KC, 128], F32, kind="ExternalInput")
    bbd = nc.dram_tensor("bbd", [128, KC], F32, kind="ExternalInput")
    encd = nc.dram_tensor("encd", [128, KC, 40], F32, kind="ExternalInput")
    cfd = nc.dram_tensor("cfd", [128, KC, 3, 40], F32, kind="ExternalInput")
    mkd = nc.dram_tensor("mkd", [128, KC, 3, 40], U8, kind="ExternalInput")
    gtd = nc.dram_tensor("gtd", [D, D], BF16, kind="ExternalInput")
    seld = nc.dram_tensor("seld", [HKV, HKV, 128], BF16, kind="ExternalInput")
    onescd = nc.dram_tensor("onescd", [128, 1], BF16, kind="ExternalInput")
    onesrd = nc.dram_tensor("onesrd", [1, 128], BF16, kind="ExternalInput")
    epslnd = nc.dram_tensor("epslnd", [1, 1], F32, kind="ExternalInput")
    epsdend = nc.dram_tensor("epsdend", [HKV, 1], F32, kind="ExternalInput")
    oT = nc.dram_tensor("oT", [D, NTOK], F32, kind="ExternalOutput")
    wofs = nc.dram_tensor("wofs", [KC, 128, HKV, 128], BF16, kind="Internal")
    if DBG:
        dWc = nc.dram_tensor("dWc", [128, KC * 40], BF16, kind="ExternalOutput")
        dAt = nc.dram_tensor("dAt", [40, D], BF16, kind="ExternalOutput")
        dEbf = nc.dram_tensor("dEbf", [40, TT], BF16, kind="ExternalOutput")
        dDen = nc.dram_tensor("dDen", [HKV, TT], F32, kind="ExternalOutput")
        dU = nc.dram_tensor("dU", [128, KC * TT], BF16, kind="ExternalOutput")
        dRetr = nc.dram_tensor("dRetr", [128, HKV * TT], BF16, kind="ExternalOutput")
        dY = nc.dram_tensor("dY", [128, KC * TT], BF16, kind="ExternalOutput")
        dF = nc.dram_tensor("dF", [128, KC * TT], BF16, kind="ExternalOutput")
        dV = nc.dram_tensor("dV", [128, KC * TT], BF16, kind="ExternalOutput")
        dSq = nc.dram_tensor("dSq", [HKV, 128, TT], BF16, kind="ExternalOutput")
        dWofs = nc.dram_tensor("dWofs", [KC, 128, HKV, 128], BF16, kind="ExternalOutput")
    DBGG = {"done": 0}
    dump_u = [0] * KC
    dump_y = [0] * KC

    A = nc.alloc_sbuf_tensor
    xbuf = A("xbuf", [128, KC * TT], BF16)
    freg = A("freg", [128, KC * TT], BF16)
    wb_sb = [A(f"wb{i}", [128, KC * 128], BF16) for i in range(2)]
    usq = [A(f"usq{i}", [128, TT], BF16) for i in range(2)]
    outf = [A(f"outf{i}", [128, TT], F32) for i in range(2)]
    m_bf = A("m_bf", [128, HKV * 128], BF16)
    zm_bf = A("zm_bf", [128, HKV * HKV], BF16)
    wcat_f = A("wcat_f", [128, KC * 40], F32)
    wcat_bf = A("wcat_bf", [128, KC * 40], BF16)
    enc_bf = A("enc_bf", [128, KC * 40], BF16)
    at_half = A("at_half", [40, D], BF16)
    sel_sb = A("sel_sb", [HKV, HKV * 128], BF16)
    e_bf = A("e_bf", [40, TT], BF16)
    den_sb = A("den_sb", [HKV, TT], F32)
    rden_bf = A("rden_bf", [HKV, TT], BF16)
    rows = A("rows", [1, 8 * 512], F32)
    rowsb = A("rowsb", [1, 4 * 512], BF16)
    bq_sb = A("bq_sb", [128, HKV], F32)
    bb_sb = A("bb_sb", [128, KC], F32)
    onesc = A("onesc", [128, 1], BF16)
    onesr = A("onesr", [1, 128], BF16)
    epsln = A("epsln", [1, 1], F32)
    epsden = A("epsden", [HKV, 1], F32)

    freg_addr = nc.lookup_mloc(freg).addr
    Aat = nc.alloc_sbuf_tensor_at
    retr = Aat("retr", [128, HKV * TT], BF16, offset=freg_addr + 0)
    wq_sb = [Aat(f"wqs{i}", [128, KC * 128], BF16, offset=freg_addr + 16384 + i * 8192)
             for i in range(2)]
    wofm = [Aat(f"wofm{i}", [128, HKV * 128], BF16, offset=freg_addr + 32768 + i * 2048)
            for i in range(2)]
    sq_sb = [Aat(f"sqs{i}", [128, TT], BF16, offset=freg_addr + 36864 + i * 2048)
             for i in range(2)]
    cst = [Aat(f"cst{i}", [128, 8 * 120], F32, offset=freg_addr + 40960 + i * 3840)
           for i in range(2)]
    mst = [Aat(f"mst{i}", [128, 8 * 120], U8, offset=freg_addr + 48640 + i * 960)
           for i in range(2)]
    fbuf = [[Aat(f"fb{s}_{i}", [128, 512], BF16,
                 offset=freg_addr + 40960 + (s * 4 + i) * 1024)
             for i in range(4)] for s in range(4)]
    mkf = Aat("mkf", [128, 8 * 120], F32, offset=freg_addr + 58752)
    wtmp = Aat("wtmp", [128, 320], F32, offset=freg_addr + 62592)
    usq0a = nc.lookup_mloc(usq[0]).addr
    usq1a = nc.lookup_mloc(usq[1]).addr
    mscr = [Aat("mscr0", [128, 512], F32, offset=usq0a),
            Aat("mscr1", [128, 512], F32, offset=usq1a)]
    rb_sb = A("rb_sb", [128, 512], BF16)

    P = [nc.alloc_psum_tensor(f"P{i}", [128, TT], F32) for i in range(4)]

    QK = ["t", "v", "a", "s", "g"]
    Q = {k: [] for k in QK}
    CNT = {k: 0 for k in ALLSEMS}

    def emit(eng, body, waits=(), inc=False, tag=""):
        isem = "p" if eng == "g" else eng
        if inc:
            assert isem in ENG
            CNT[isem] += 1
        filtered = tuple((wk, wv) for (wk, wv) in waits
                         if isinstance(wv, dict) or wv > 0)
        META.append((eng, filtered, isem if inc else None, 1, tag))

        def f(E, SEMS, body=body, filtered=filtered, inc=inc, isem=isem):
            for (wk, wv) in filtered:
                v = wv["v"] if isinstance(wv, dict) else wv
                if v > 0:
                    E.wait_ge(SEMS[wk], v)
            ins = body(E)
            if inc:
                ins.then_inc(SEMS[isem], 1)
        Q[eng].append(f)
        return CNT[isem] if isem in ENG else None

    def dma(qeng, semk, dst, src, waits=()):
        CNT[semk] += 16
        filtered = tuple((wk, wv) for (wk, wv) in waits
                         if isinstance(wv, dict) or wv > 0)
        META.append((qeng, filtered, semk, 16, "dma"))

        def f(E, SEMS, dst=dst, src=src, filtered=filtered, semk=semk):
            for (wk, wv) in filtered:
                v = wv["v"] if isinstance(wv, dict) else wv
                if v > 0:
                    E.wait_ge(SEMS[wk], v)
            E.dma_start(dst, src).then_inc(SEMS[semk], 16)
        Q[qeng].append(f)
        return CNT[semk]

    def barrier(bit):
        if not (SERIAL & bit):
            return
        snap = dict(CNT)
        for e in QK:
            emit(e, lambda E: E.nop(),
                 [(k2, snap[k2]) for k2 in ALLSEMS if k2 != e], inc=(e in ENG))

    dma("s", "c", bq_sb[:], bqd[:])
    dma("s", "c", bb_sb[:], bbd[:])
    dma("s", "c", sel_sb[:], seld.rearrange("r h p -> r (h p)"))
    dma("s", "c", onesc[:], onescd[:])
    dma("s", "c", onesr[:], onesrd[:])
    dma("s", "c", epsln[:], epslnd[:])
    C_TOT = dma("s", "c", epsden[:], epsdend[:])
    dma("g", "m", m_bf[:], mhd.rearrange("p h e -> p (h e)"))
    dma("g", "m", zm_bf[:], zmd.rearrange("p h r -> p (h r)"))
    M_TOT = dma("g", "m", enc_bf[:], encd.rearrange("p k r -> p (k r)"))

    cslice = lambda t_, e: t_.rearrange("p (k e r) -> p k e r", k=8, e=3, r=40)[:, :, e, :]
    mslab3 = mkf.rearrange("p (k e r) -> p k e r", k=8, e=3, r=40)
    wtmp3 = wtmp.rearrange("p (k r) -> p k r", k=8, r=40)
    v_wg = [0] * 4
    for g in range(4):
        pg = g % 2
        w_war = [("v", v_wg[g - 1])] if g >= 1 else []
        c_cf = dma("s", "cf", cst[pg][:],
                   cfd[:, g * 8:(g + 1) * 8].rearrange("p k e r -> p (k e r)"),
                   waits=w_war)
        c_mk = dma("s", "mk", mst[pg][:],
                   mkd[:, g * 8:(g + 1) * 8].rearrange("p k e r -> p (k e r)"),
                   waits=w_war)
        wslab = wcat_f.rearrange("p (k r) -> p k r", k=KC, r=40)[:, g * 8:(g + 1) * 8, :]
        vc = emit("v", lambda E, pg=pg: E.tensor_copy(mkf[:], mst[pg][:]),
                  [("cf", c_cf), ("mk", c_mk)], inc=True)
        vc = emit("v", lambda E, pg=pg, wslab=wslab: E.tensor_tensor(
            wslab, cslice(cst[pg], 0), mslab3[:, :, 0, :], ALU.mult),
            [("v", vc)], inc=True)
        for e in (1, 2):
            vc = emit("v", lambda E, pg=pg, e=e: E.tensor_tensor(
                wtmp3, cslice(cst[pg], e), mslab3[:, :, e, :], ALU.mult),
                [("v", vc)], inc=True)
            vc = v_wg[g] = emit("v", lambda E, wslab=wslab: E.tensor_tensor(
                wslab, wslab, wtmp3, ALU.add), [("v", vc)], inc=True)
    v_wcat = emit("v", lambda E: E.tensor_copy(wcat_bf[:], wcat_f[:]),
                  [("v", v_wg[3])], inc=True)
    if DBG:
        dma("s", "db", dWc[:], wcat_bf[:], waits=[("v", v_wcat)])

    NPIECE = 64

    class Fold:
        def __init__(self):
            self.load_cnt = [0] * NPIECE
            self.add_cell = [{"v": 0} for _ in range(NPIECE)]
            self.write_cell = [{"v": 0} for _ in range(NPIECE)]

        def emit_loads(self, p):
            c8, kv = p // 8, p % 8
            s = p % 4
            semk = f"fo{s}"
            cols = slice(c8 * 512, (c8 + 1) * 512)
            for gg in range(4):
                if p < 4:
                    w = [("v", v_wcat)]
                elif gg == 0:
                    w = [(f"fw{s}", self.write_cell[p - 4])]
                else:
                    w = [("p", self.add_cell[p - 4])] if gg == 1 else []
                self.load_cnt[p] = dma("g", semk, fbuf[s][gg][:],
                                       wod[(kv * 4 + gg) * 128:(kv * 4 + gg + 1) * 128,
                                           cols],
                                       waits=w)

        def emit_adds_write(self, p):
            c8, kv = p // 8, p % 8
            s = p % 4
            semk = f"fo{s}"
            vv = emit("g", lambda E, s=s: E.tensor_add(
                fbuf[s][0][:], fbuf[s][0][:], fbuf[s][1][:]),
                [(semk, self.load_cnt[p])], inc=True)
            vv = emit("g", lambda E, s=s: E.tensor_add(
                fbuf[s][0][:], fbuf[s][0][:], fbuf[s][2][:]), [("p", vv)], inc=True)
            vv = emit("g", lambda E, s=s: E.tensor_add(
                fbuf[s][0][:], fbuf[s][0][:], fbuf[s][3][:]), [("p", vv)], inc=True)
            self.add_cell[p]["v"] = vv
            self.write_cell[p]["v"] = dma(
                "s", f"fw{s}",
                wofs[c8 * 4:(c8 + 1) * 4, :, kv, :].rearrange("m p c -> p m c"),
                fbuf[s][0].rearrange("p (m c) -> p m c", m=4, c=128),
                waits=[("p", vv)])

    fold = Fold()


    out_tt_done = [0] * KC
    outdma_done = [0] * KC
    v_done_all = 0
    q_done = {}
    f_done = {}
    fev_done = {}
    aprep_T = 0
    aprep_A = 0
    prev_sq_gate = [0] * KC
    prev_out_tt_all = 0

    for t in range(NTILES):
        tok0 = t * TT
        barrier(1024)

        xg_tot = [0] * 4
        wq_dma = {}
        for k in range(KC):
            w = [("v", out_tt_done[k])] if t > 0 else []
            if DBG and t > 0:
                w.append(("db", DBGG["done"]))
            xg_tot[k // 8] = dma("g", f"x{k // 8}", xbuf[:, k * TT:(k + 1) * TT],
                                 xT[k * 128:(k + 1) * 128, tok0:tok0 + TT], waits=w)
            if k % 4 == 3 and k // 4 < 2:
                h = k // 4
                wqw = []
                if t > 0:
                    wqw.append(("t", q_done.get((t - 1, h + HKV - 2), 0)))
                    wqw.append(("v", v_done_all))
                    if DBG:
                        wqw.append(("db", DBGG["done"]))
                wq_dma[h] = dma("g", f"wq{h % 2}", wq_sb[h % 2][:],
                                wqd[h].rearrange("p k o -> p (k o)"), waits=wqw)

        wqcell = {}
        if t == 0:
            WQPOS = {0: 2, 4: 3, 8: 4, 12: 5, 16: 6, 20: 7}
            for p in range(64):
                fold.emit_loads(p)
                if p >= 3:
                    fold.emit_adds_write(p - 3)
                if p in WQPOS:
                    h = WQPOS[p]
                    wqcell[h] = {"v": 0}
                    wq_dma[h] = dma("g", f"wq{h % 2}", wq_sb[h % 2][:],
                                    wqd[h].rearrange("p k o -> p (k o)"),
                                    waits=[("t", wqcell[h])])
            for p in range(61, 64):
                fold.emit_adds_write(p)

        dump_sq = {}
        elu_done = {}
        num_T = {}
        den_T = 0
        numev_A = {}

        def emit_q(h):
            Pq = P[h % 2]
            last = 0
            for k in range(KC):
                wts = [(f"x{k // 8}", xg_tot[k // 8])] if k % 8 == 0 else []
                if k == 0:
                    wts.append((f"wq{h % 2}", wq_dma[h]))
                    if h >= 2:
                        wts.append(("v", elu_done[(h - 2, 1)]))
                    elif t > 0:
                        wts.append(("a", fev_done[(t - 1, 30 + (h % 2))]))
                for nt in range(NT2):
                    last = emit("t", lambda E, Pq=Pq, h=h, k=k, nt=nt: E.matmul(
                        Pq[:, nt * 512:(nt + 1) * 512],
                        wq_sb[h % 2][:, k * 128:(k + 1) * 128],
                        xbuf[:, k * TT + nt * 512: k * TT + (nt + 1) * 512],
                        start=(k == 0), stop=(k == KC - 1)),
                        waits=wts if nt == 0 else (), inc=(k == KC - 1))
            q_done[(t, h)] = last

        def emit_wq_prefetch(h):
            if t == 0:
                wqcell[h]["v"] = q_done[(t, h - 2)]
            else:
                wq_dma[h] = dma("g", f"wq{h % 2}", wq_sb[h % 2][:],
                                wqd[h].rearrange("p k o -> p (k o)"),
                                waits=[("t", q_done[(t, h - 2)])])

        def emit_elu(h):
            Pq = P[h % 2]
            for nt in range(NT2):
                wts = [("t", q_done[(t, h)])]
                if h == 0 and nt == 0:
                    wts.append(("c", C_TOT))
                    wts.append(("t", prev_sq_gate[KC - 1]))
                if h >= 2 and nt == 0:
                    wts.append(("t", den_T))
                if SERIAL & 8192 and h >= 1 and nt == 0:
                    wts.append(("a", numev_A.get(h - 1, 0)))
                emit("v", lambda E, Pq=Pq, h=h, nt=nt: E.tensor_scalar(
                    mscr[nt][:], Pq[:, nt * 512:(nt + 1) * 512],
                    bq_sb[:, h:h + 1], 0.0, ALU.add, ALU.min), wts, inc=True)
                vaw = [("v", CNT["v"])]
                if DBG and t == 0 and h >= 2 and nt == 0:
                    vaw.append(("db", dump_sq[h - 2]))
                if DBG and t == 1 and h == 0 and nt == 0:
                    vaw.append(("db", DBGG["done"]))
                va = emit("a", lambda E, h=h, nt=nt: E.activation(
                    sq_sb[h % 2][:, nt * 512:(nt + 1) * 512], mscr[nt][:], AF.Exp),
                    vaw, inc=True)
                v_rb = emit("v", lambda E, Pq=Pq, h=h, nt=nt: E.tensor_scalar(
                    rb_sb[:], Pq[:, nt * 512:(nt + 1) * 512],
                    bq_sb[:, h:h + 1], 0.0, ALU.add, ALU.max),
                    [("v", CNT["v"])], inc=True)
                elu_done[(h, nt)] = emit("v", lambda E, h=h, nt=nt: E.tensor_tensor(
                    sq_sb[h % 2][:, nt * 512:(nt + 1) * 512],
                    sq_sb[h % 2][:, nt * 512:(nt + 1) * 512],
                    rb_sb[:], ALU.add), [("a", va), ("v", v_rb)], inc=True)
            if DBG and t == 0:
                dump_sq[h] = dma("s", "db", dSq[h], sq_sb[h % 2][:],
                                 waits=[("v", elu_done[(h, 1)])])

        def emit_numden(h):
            nonlocal den_T
            for nt in range(NT2):
                wts = [("v", elu_done[(h, nt)])]
                if nt == 0:
                    if h > 0:
                        wts.append(("a", numev_A[h - 1]))
                    elif t > 0:
                        wts.append(("v", prev_out_tt_all))
                    if h == 0 and t == 0:
                        wts.append(("m", M_TOT))
                num_T[(h, nt)] = emit("t", lambda E, h=h, nt=nt: E.matmul(
                    P[2][:, nt * 512:(nt + 1) * 512],
                    m_bf[:, h * 128:(h + 1) * 128],
                    sq_sb[h % 2][:, nt * 512:(nt + 1) * 512],
                    start=True, stop=True), wts, inc=True)
            for nt in range(NT2):
                wts = []
                if h == 0 and nt == 0:
                    if t == 0:
                        wts.append(("m", M_TOT))
                    if t > 0:
                        wts.append(("v", prev_out_tt_all))
                den_T = emit("t", lambda E, h=h, nt=nt: E.matmul(
                    P[3][0:HKV, nt * 512:(nt + 1) * 512],
                    zm_bf[:, h * HKV:(h + 1) * HKV],
                    sq_sb[h % 2][:, nt * 512:(nt + 1) * 512],
                    start=(h == 0), stop=(h == HKV - 1)), wts, inc=True)
            for nt in range(NT2):
                nev = [("t", num_T[(h, nt)])]
                if DBG and t == 1 and h == 0 and nt == 0:
                    nev.append(("db", DBGG["done"]))
                numev_A[h] = emit("a", lambda E, h=h, nt=nt: E.activation(
                    retr[:, h * TT + nt * 512: h * TT + (nt + 1) * 512],
                    P[2][:, nt * 512:(nt + 1) * 512], AF.Copy),
                    nev, inc=True)

        if SERIAL & 8192:
            for h in range(HKV):
                emit_q(h)
                if h + 2 < HKV:
                    emit_wq_prefetch(h + 2)
                emit_elu(h)
                if t == 0:
                    for p in range(4 * h, min(4 * h + 4, 32)):
                        fold.emit_adds_write(p)
                emit_numden(h)

        else:
            emit_q(0)
            emit_wq_prefetch(2)
            emit_elu(0)
            for h in range(1, HKV):
                emit_q(h)
                if h + 2 < HKV:
                    emit_wq_prefetch(h + 2)
                emit_elu(h)
                emit_numden(h - 1)
            emit_numden(HKV - 1)

        barrier(1)

        if DBG and t == 0:
            dma("s", "db", dWofs[:], wofs[:], waits=[("fw0", 256), ("fw1", 256)])
        a_den = emit("a", lambda E: E.activation(
            den_sb[:], P[3][0:HKV, :], AF.Copy, bias=float(EPS_DEN)),
            [("t", den_T)], inc=True)
        dden_c = 0
        if DBG and t == 0:
            dden_c = dma("s", "db", dDen[:], den_sb[:], waits=[("a", a_den)])
        v_rec = emit("v", lambda E: E.reciprocal(den_sb[:], den_sb[:]),
                     [("a", a_den)] + ([("db", dden_c)] if (DBG and t == 0) else []),
                     inc=True)
        a_rden = emit("a", lambda E: E.activation(rden_bf[:], den_sb[:], AF.Copy),
                      [("v", v_rec)], inc=True)
        retr_V = 0
        for h in range(HKV):
            bc_T = {}
            for nt in range(NT2):
                wts = [("a", a_rden), ("c", C_TOT)]
                if h > 0:
                    wts.append(("v", retr_V))
                bc_T[nt] = emit("t", lambda E, h=h, nt=nt: E.matmul(
                    P[3][:, nt * 512:(nt + 1) * 512],
                    sel_sb[:, h * 128:(h + 1) * 128],
                    rden_bf[0:HKV, nt * 512:(nt + 1) * 512],
                    start=True, stop=True), wts, inc=True)
            for nt in range(NT2):
                retr_V = emit("v", lambda E, h=h, nt=nt: E.tensor_tensor(
                    retr[:, h * TT + nt * 512: h * TT + (nt + 1) * 512],
                    retr[:, h * TT + nt * 512: h * TT + (nt + 1) * 512],
                    P[3][:, nt * 512:(nt + 1) * 512], ALU.mult),
                    [("t", bc_T[nt])], inc=True)
        barrier(2)
        if DBG and t == 0:
            dma("s", "db", dRetr[:], retr[:], waits=[("v", retr_V)])

        wofm_dma = {}
        wof_T = {}
        u_done = [0] * KC
        for m in range(2):
            wts = []
            if t == 0:
                wts += [(f"fw{i}", 16 * 16) for i in range(4)]
            if t > 0:
                wts.append(("v", v_done_all))
                if DBG:
                    wts.append(("db", DBGG["done"]))
            wofm_dma[m] = dma("s", f"wf{m % 2}", wofm[m % 2][:],
                              wofs[m].rearrange("p kv c -> p (kv c)"), waits=wts)
        for m in range(KC):
            last = 0
            for kv in range(HKV):
                wts = []
                if kv == 0:
                    wts = [(f"wf{m % 2}", wofm_dma[m]), ("v", retr_V)]
                    if m >= 2:
                        wts.append(("v", u_done[m - 2]))
                    elif t > 0:
                        wts.append(("a", fev_done[(t - 1, 30 + (m % 2))]))
                for nt in range(NT2):
                    last = emit("t", lambda E, m=m, kv=kv, nt=nt: E.matmul(
                        P[m % 2][:, nt * 512:(nt + 1) * 512],
                        wofm[m % 2][:, kv * 128:(kv + 1) * 128],
                        retr[:, kv * TT + nt * 512: kv * TT + (nt + 1) * 512],
                        start=(kv == 0), stop=(kv == HKV - 1)),
                        waits=wts if nt == 0 else (), inc=(kv == HKV - 1))
            wof_T[m] = last
            if m + 2 < KC:
                wofm_dma[m + 2] = dma("s", f"wf{m % 2}", wofm[m % 2][:],
                                      wofs[m + 2].rearrange("p kv c -> p (kv c)"),
                                      waits=[("t", wof_T[m])])
            u_done[m] = emit("v", lambda E, m=m: E.tensor_tensor(
                xbuf[:, m * TT:(m + 1) * TT], xbuf[:, m * TT:(m + 1) * TT],
                P[m % 2][:], ALU.add), [("t", wof_T[m])], inc=True)
            if DBG and t == 0:
                dump_u[m] = dma("s", "db", dU[:, m * TT:(m + 1) * TT],
                                xbuf[:, m * TT:(m + 1) * TT],
                                waits=[("v", u_done[m])])
        barrier(4)

        if t == 0:
            apmm_k = [0] * KC
            G_dma = [0] * KC
            G_dma[0] = dma("g", "ga", wb_sb[0][:], gtd[0:128, :])
            G_dma[1] = dma("g", "gb", wb_sb[1][:], gtd[128:256, :])
            for k in range(KC):
                wts = [(("ga" if k % 2 == 0 else "gb"), G_dma[k])]
                if k == 0:
                    wts += [("v", max(u_done)), ("v", v_wcat)]
                for jp in range(8):
                    apmm_k[k] = emit("t", lambda E, k=k, jp=jp: E.matmul(
                        P[jp // 2][0:40, (jp % 2) * 512:(jp % 2 + 1) * 512],
                        wcat_bf[:, k * 40:(k + 1) * 40],
                        wb_sb[k % 2][:, jp * 512:(jp + 1) * 512],
                        start=(k == 0), stop=(k == KC - 1)),
                        waits=wts if jp == 0 else (), inc=True)
                if k + 2 < KC:
                    G_dma[k + 2] = dma("g", ("ga" if k % 2 == 0 else "gb"),
                                       wb_sb[k % 2][:],
                                       gtd[(k + 2) * 128:(k + 3) * 128, :],
                                       waits=[("t", apmm_k[k])])
            aprep_T = apmm_k[KC - 1]
            for jp in range(8):
                aprep_A = emit("a", lambda E, jp=jp: E.activation(
                    at_half[:, jp * 512:(jp + 1) * 512],
                    P[jp // 2][0:40, (jp % 2) * 512:(jp % 2 + 1) * 512],
                    AF.Copy, scale=0.5), [("t", aprep_T)], inc=True)
            if DBG:
                dma("s", "db", dAt[:], at_half[:], waits=[("a", aprep_A)])
        barrier(8)

        elu_last_v = elu_done[(HKV - 1, 1)]

        def layernorm(src_done, first_gate_A, usq_prev_v=0):
            mean_T = [0, 0]
            sq_T = [0, 0]
            sqmm_k = [0] * KC
            usq_A = [0] * KC
            for k in range(KC):
                wts_u = [("v", src_done[k])]
                if k == 0:
                    wts_u.append(("c", C_TOT))
                    wts_u.append(("a", first_gate_A))
                for nt in range(NT2):
                    mean_T[nt] = emit("t", lambda E, k=k, nt=nt: E.matmul(
                        P[2][0:1, nt * 512:(nt + 1) * 512], onesc[:],
                        xbuf[:, k * TT + nt * 512: k * TT + (nt + 1) * 512],
                        start=(k == 0), stop=(k == KC - 1)),
                        waits=wts_u if nt == 0 else (), inc=(k == KC - 1))
                wsq = [("v", src_done[k]),
                       ("t", prev_sq_gate[KC - 2 + k] if k < 2 else sqmm_k[k - 2])]
                if k < 2 and usq_prev_v:
                    wsq.append(("v", usq_prev_v))
                usq_A[k] = emit("a", lambda E, k=k: E.activation(
                    usq[k % 2][:], xbuf[:, k * TT:(k + 1) * TT], AF.Square),
                    wsq, inc=True)
                for nt in range(NT2):
                    sq_T[nt] = emit("t", lambda E, k=k, nt=nt: E.matmul(
                        P[3][0:1, nt * 512:(nt + 1) * 512], onesc[:],
                        usq[k % 2][:, nt * 512:(nt + 1) * 512],
                        start=(k == 0), stop=(k == KC - 1)),
                        waits=[("a", usq_A[k])] if nt == 0 else (),
                        inc=(nt == NT2 - 1 or k == KC - 1))
                sqmm_k[k] = sq_T[NT2 - 1]
            R = lambda i: rows[0:1, i * 512:(i + 1) * 512]
            a_mean = [0, 0]
            for nt in range(NT2):
                a_mean[nt] = emit("a", lambda E, nt=nt: E.activation(
                    R(0 + nt), P[2][0:1, nt * 512:(nt + 1) * 512], AF.Copy,
                    scale=1.0 / D), [("t", mean_T[nt])], inc=True)
                emit("a", lambda E, nt=nt: E.activation(
                    R(2 + nt), P[3][0:1, nt * 512:(nt + 1) * 512], AF.Copy,
                    scale=1.0 / D), [("t", sq_T[nt])], inc=True)
            a_msq = CNT["a"]
            v_mm = emit("v", lambda E: (
                E.tensor_mul(R(4), R(0), R(0)),
                E.tensor_mul(R(5), R(1), R(1)))[-1], [("a", a_msq)], inc=True)
            v_var = emit("v", lambda E: (
                E.tensor_sub(R(2), R(2), R(4)),
                E.tensor_sub(R(3), R(3), R(5)))[-1], [("v", v_mm)], inc=True)
            a_ln = emit("a", lambda E: (
                E.activation(R(4), R(2), AF.Ln, bias=epsln[:]),
                E.activation(R(5), R(3), AF.Ln, bias=epsln[:]))[-1],
                [("v", v_var), ("c", C_TOT)], inc=True)
            a_rstd = emit("a", lambda E: (
                E.activation(R(6), R(4), AF.Exp, scale=-0.5),
                E.activation(R(7), R(5), AF.Exp, scale=-0.5))[-1],
                [("a", a_ln)], inc=True)
            v_brow = emit("v", lambda E: (
                E.tensor_mul(R(4), R(0), R(6)),
                E.tensor_mul(R(5), R(1), R(7)))[-1], [("a", a_rstd)], inc=True)
            Rb = lambda i: rowsb[0:1, i * 512:(i + 1) * 512]
            a_rowsb = emit("a", lambda E: (
                E.activation(Rb(0), R(6), AF.Copy),
                E.activation(Rb(1), R(7), AF.Copy),
                E.activation(Rb(2), R(4), AF.Copy),
                E.activation(Rb(3), R(5), AF.Copy))[-1], [("v", v_brow)], inc=True)
            return a_rowsb, sqmm_k, max(a_mean)

        ln1_rowsb, ln1_sqmm, ln1_meanA = layernorm(
            u_done, aprep_A if t == 0 else 0, usq_prev_v=elu_last_v)
        prev_sq_gate = ln1_sqmm
        bc1_T = 0
        for nt in range(NT2):
            bc1_T = emit("t", lambda E, nt=nt: E.matmul(
                P[0][:, nt * 512:(nt + 1) * 512], onesr[:],
                rowsb[0:1, nt * 512:(nt + 1) * 512], start=True, stop=True),
                [("a", ln1_rowsb), ("c", C_TOT)] if nt == 0 else (), inc=True)
            bc1_T = emit("t", lambda E, nt=nt: E.matmul(
                P[1][:, nt * 512:(nt + 1) * 512], onesr[:],
                rowsb[0:1, (2 + nt) * 512:(3 + nt) * 512], start=True, stop=True),
                (), inc=True)
        norm_done = [0] * KC
        for k in range(KC):
            nw = [("t", bc1_T)] if k == 0 else []
            if DBG and t == 0:
                nw.append(("db", dump_u[k]))
            v_mul = emit("v", lambda E, k=k: E.tensor_tensor(
                xbuf[:, k * TT:(k + 1) * TT], xbuf[:, k * TT:(k + 1) * TT],
                P[0][:], ALU.mult), nw, inc=True)
            norm_done[k] = emit("v", lambda E, k=k: E.tensor_tensor(
                xbuf[:, k * TT:(k + 1) * TT], xbuf[:, k * TT:(k + 1) * TT],
                P[1][:], ALU.subtract), [("v", v_mul)], inc=True)
            if DBG and t == 0:
                dump_y[k] = dma("s", "db", dY[:, k * TT:(k + 1) * TT],
                                xbuf[:, k * TT:(k + 1) * TT],
                                waits=[("v", norm_done[k])])
        barrier(16)

        e_T = 0
        for k in range(KC):
            wts = [("v", norm_done[k])]
            if k == 0:
                wts += [("m", M_TOT), ("a", ln1_meanA)]
            for nt in range(NT2):
                e_T = emit("t", lambda E, k=k, nt=nt: E.matmul(
                    P[2][0:40, nt * 512:(nt + 1) * 512],
                    enc_bf[:, k * 40:(k + 1) * 40],
                    xbuf[:, k * TT + nt * 512: k * TT + (nt + 1) * 512],
                    start=(k == 0), stop=(k == KC - 1)),
                    waits=wts if nt == 0 else (), inc=(k == KC - 1))
        ebf_A = emit("a", lambda E: E.activation(e_bf[:], P[2][0:40, :], AF.Copy),
                     [("t", e_T)], inc=True)
        if DBG and t == 0:
            dma("s", "db", dEbf[:], e_bf[:], waits=[("a", ebf_A)])
        barrier(32)
        wb_dma = {}

        def emit_wb(m, wts):
            semk = f"wb{m % 2}"
            flat = wbd[m].rearrange("p k o -> p (k o)")
            h0 = dma("g", semk, wb_sb[m % 2][:, 0:2048], flat[:, 0:2048], waits=wts)
            h1 = dma("g", semk, wb_sb[m % 2][:, 2048:4096], flat[:, 2048:4096])
            wb_dma[m] = (h0, h1)

        for m in range(2):
            wts = [("t", aprep_T)] if t == 0 else \
                [("t", f_done.get((t - 1, KC - 2 + m), 0))]
            emit_wb(m, wts)
        for m in range(KC):
            Pf = P[m % 2]
            last = 0
            for nt in range(NT2):
                wts = ()
                if nt == 0:
                    wts = [("a", ebf_A), (f"wb{m % 2}", wb_dma[m][0])]
                    if m < 2:
                        wts.append(("v", norm_done[KC - 1]))
                    else:
                        wts.append(("a", fev_done[(t, m - 2)]))
                    if SERIAL & 4096 and m >= 1:
                        wts.append(("a", fev_done[(t, m - 1)]))
                emit("t", lambda E, Pf=Pf, m=m, nt=nt: E.matmul(
                    Pf[:, nt * 512:(nt + 1) * 512],
                    at_half[:, m * 128:(m + 1) * 128],
                    e_bf[0:40, nt * 512:(nt + 1) * 512],
                    start=True, stop=False), waits=wts)
                for k in range(KC):
                    kw = [(f"wb{m % 2}", wb_dma[m][1])] if (nt == 0 and k == 16) else ()
                    last = emit("t", lambda E, Pf=Pf, m=m, k=k, nt=nt: E.matmul(
                        Pf[:, nt * 512:(nt + 1) * 512],
                        wb_sb[m % 2][:, k * 128:(k + 1) * 128],
                        xbuf[:, k * TT + nt * 512: k * TT + (nt + 1) * 512],
                        start=False, stop=(k == KC - 1)),
                        waits=kw, inc=(k == KC - 1))
            f_done[(t, m)] = last
            fev_done[(t, m)] = emit("a", lambda E, m=m: E.activation(
                freg[:, m * TT:(m + 1) * TT], P[m % 2][:], AF.Copy),
                [("t", f_done[(t, m)])], inc=True)
            if m + 2 < KC:
                wbw = [("t", f_done[(t, m)])]
                if SERIAL & 4096:
                    wbw.append(("a", fev_done[(t, m)]))
                emit_wb(m + 2, wbw)
        barrier(64)
        vv_done = [0] * KC
        for m in range(KC):
            vw = [("a", fev_done[(t, m)]), ("t", f_done[(t, KC - 1)])]
            if m == 0:
                vw.append(("c", C_TOT))
            if DBG and t == 0:
                vw.append(("db", dump_y[m]))
                dma("s", "db", dF[:, m * TT:(m + 1) * TT], freg[:, m * TT:(m + 1) * TT],
                    waits=[("a", fev_done[(t, m)])])
            vv_done[m] = emit("v", lambda E, m=m: E.scalar_tensor_tensor(
                xbuf[:, m * TT:(m + 1) * TT], freg[:, m * TT:(m + 1) * TT],
                bb_sb[:, m:m + 1], xbuf[:, m * TT:(m + 1) * TT],
                ALU.add, ALU.add), vw, inc=True)
            if DBG and t == 0:
                dma("s", "db", dV[:, m * TT:(m + 1) * TT], xbuf[:, m * TT:(m + 1) * TT],
                    waits=[("v", vv_done[m])])
        v_done_all = vv_done[KC - 1]
        barrier(128)

        ln2_rowsb, ln2_sqmm, _ = layernorm(vv_done, ebf_A)
        prev_sq_gate = ln2_sqmm
        barrier(256)
        bc2_T = 0
        for nt in range(NT2):
            bc2_T = emit("t", lambda E, nt=nt: E.matmul(
                P[2][:, nt * 512:(nt + 1) * 512], onesr[:],
                rowsb[0:1, nt * 512:(nt + 1) * 512], start=True, stop=True),
                [("a", ln2_rowsb)] if nt == 0 else (), inc=True)
            bc2_T = emit("t", lambda E, nt=nt: E.matmul(
                P[3][:, nt * 512:(nt + 1) * 512], onesr[:],
                rowsb[0:1, (2 + nt) * 512:(3 + nt) * 512], start=True, stop=True),
                (), inc=True)
        new_outdma = [0] * KC
        for k in range(KC):
            wts = [("t", bc2_T)] if k == 0 else []
            ping_gate = CNT["out"] if (k >= 2 or t > 0) else 0
            if ping_gate:
                wts.append(("out", ping_gate))
            v_mul2 = emit("v", lambda E, k=k: E.tensor_tensor(
                outf[k % 2][:], xbuf[:, k * TT:(k + 1) * TT], P[2][:], ALU.mult),
                wts, inc=True)
            out_tt_done[k] = emit("v", lambda E, k=k: E.tensor_tensor(
                outf[k % 2][:], outf[k % 2][:], P[3][:], ALU.subtract),
                [("v", v_mul2)], inc=True)
            new_outdma[k] = dma("s", "out", oT[k * 128:(k + 1) * 128, tok0:tok0 + TT],
                                outf[k % 2][:], waits=[("v", out_tt_done[k])])
        barrier(512)
        outdma_done = new_outdma
        prev_out_tt_all = out_tt_done[KC - 1]
        if DBG and t == 0:
            DBGG["done"] = CNT["db"]

    emit("s", lambda E: E.nop(), [(k, CNT[k]) for k in DMAS if CNT[k] > 0])

    with ExitStack() as ctx:
        sems = {k: ctx.enter_context(nc.semaphore(f"S_{k}")) for k in ALLSEMS}
        block = ctx.enter_context(nc.Block(no_gpsimd_drain=True))
        block.tensor(lambda E: [f(E, sems) for f in Q["t"]])
        block.vector(lambda E: [f(E, sems) for f in Q["v"]])
        block.scalar(lambda E: [f(E, sems) for f in Q["a"]])
        block.sync(lambda E: [f(E, sems) for f in Q["s"]])
        block.gpsimd(lambda E: [f(E, sems) for f in Q["g"]])
    return nc


def _prep_inputs(inputs):
    _ = inputs.get("L0_attn_Wk"), inputs.get("L0_attn_bk")
    _ = inputs.get("L0_attn_Wv"), inputs.get("L0_attn_bv")
    f32 = np.float32
    x = np.ascontiguousarray(np.asarray(inputs["L0_attn_hidden_states"], f32).reshape(16384, D))
    Wq = np.asarray(inputs["L0_attn_Wq"], f32)
    bq = np.asarray(inputs["L0_attn_bq"], f32)
    Wo = np.asarray(inputs["L0_attn_Wo"], f32)
    M = np.asarray(inputs["L0_attn_M"], f32)
    z = np.asarray(inputs["L0_attn_z"], f32)
    Wb = np.asarray(inputs["L0_ffn_W_base"], f32)
    bb = np.asarray(inputs["L0_ffn_b_base"], f32)
    e0 = np.asarray(inputs["L0_ffn_enc0"], f32)
    e1 = np.asarray(inputs["L0_ffn_enc1"], f32)
    c0 = np.asarray(inputs["L0_ffn_coefs0"], f32)
    c1 = np.asarray(inputs["L0_ffn_coefs1"], f32)
    m0 = np.asarray(inputs["L0_ffn_mask0"])
    m1 = np.asarray(inputs["L0_ffn_mask1"])

    Wq_sel = Wq.reshape(32, 128, D)[::4].reshape(1024, D)
    wq_blk = np.ascontiguousarray(Wq_sel.T.reshape(KC, 128, HKV, 128).transpose(2, 1, 0, 3))
    bq_blk = np.ascontiguousarray(bq.reshape(32, 128)[::4].T)
    mh_blk = np.ascontiguousarray(M.transpose(1, 0, 2))
    zm_blk = np.zeros((128, HKV, HKV), f32)
    for h in range(HKV):
        zm_blk[:, h, h] = z[h]
    woT = np.ascontiguousarray(Wo.T)
    wb_blk = np.ascontiguousarray(Wb.T.reshape(KC, 128, KC, 128).transpose(2, 1, 0, 3))
    bb_blk = np.ascontiguousarray(bb.reshape(KC, 128).T)
    enc_cat = np.concatenate([e0, e1], axis=0)
    enc_blk = np.ascontiguousarray(enc_cat.T.reshape(KC, 128, 40).transpose(1, 0, 2))
    ccat = np.concatenate([c0.transpose(1, 0, 2), c1.transpose(1, 0, 2)], axis=2)
    cf_blk = np.ascontiguousarray(ccat.reshape(KC, 128, 3, 40).transpose(1, 0, 2, 3))
    mcat = np.concatenate([m0.transpose(1, 0, 2), m1.transpose(1, 0, 2)], axis=2)
    mk_blk = np.ascontiguousarray(mcat.reshape(KC, 128, 3, 40).transpose(1, 0, 2, 3)).view(np.uint8)

    sel_np = np.zeros((HKV, HKV, 128), f32)
    for h in range(HKV):
        sel_np[h, h, :] = 1.0
    shared = {
        "wqd": wq_blk, "bqd": bq_blk, "mhd": mh_blk, "zmd": zm_blk, "wod": woT,
        "wbd": wb_blk, "bbd": bb_blk, "encd": enc_blk, "cfd": cf_blk, "mkd": mk_blk,
        "gtd": np.ascontiguousarray(_build_G()),
        "seld": sel_np.astype(BF),
        "onescd": np.ones((128, 1), f32).astype(BF),
        "onesrd": np.ones((1, 128), f32).astype(BF),
        "epslnd": np.full((1, 1), EPS_LN, f32),
        "epsdend": np.full((HKV, 1), EPS_DEN, f32),
    }
    in_maps = []
    for c in range(NCORES):
        m = dict(shared)
        m["xT"] = np.ascontiguousarray(x[c * NTOK:(c + 1) * NTOK].T)
        in_maps.append(m)
    return in_maps


def kernel(**inputs):
    if "nc" not in _cache:
        _cache["nc"] = build_program()
    nc = _cache["nc"]
    in_maps = _prep_inputs(inputs)
    res = run_bass_kernel_spmd(nc, in_maps, core_ids=list(range(NCORES)))
    parts = [np.asarray(res.results[c]["oT"]).T for c in range(NCORES)]
    return np.concatenate(parts, axis=0).reshape(2, 8192, D).astype(np.float32)


# version 13
# speedup vs baseline: 1.0781x; 1.0781x over previous
import numpy as np
import ml_dtypes
import concourse.bass as bass
import concourse.mybir as mybir
from concourse.bass_utils import run_bass_kernel_spmd
from contextlib import ExitStack

BF = ml_dtypes.bfloat16
BF16 = mybir.dt.bfloat16
F32 = mybir.dt.float32
U8 = mybir.dt.uint8
AF = mybir.ActivationFunctionType
ALU = mybir.AluOpType

NCORES = 8
NTOK = 2048
TT = 1024
NTILES = NTOK // TT
D = 4096
KC = 32
HKV = 8
NT2 = TT // 512
EPS_LN = 1e-5
EPS_DEN = 1e-6

_cache = {}
META = []
DBG = False
SERIAL = 0


def _build_G():
    n = D
    j = np.arange(n)[None, :].astype(np.float64)
    G = np.zeros((n, n), dtype=np.float64)
    G[0, :] = 1.0 / np.sqrt(n)
    maxf = (n - 1) // 2
    f = np.arange(1, maxf + 1)[:, None].astype(np.float64)
    ang = 2.0 * np.pi * f * j / n
    G[1:1 + 2 * maxf:2, :] = np.sqrt(2.0 / n) * np.cos(ang)
    G[2:2 + 2 * maxf:2, :] = np.sqrt(2.0 / n) * np.sin(ang)
    G[n - 1, :] = ((-1.0) ** j[0]) / np.sqrt(n)
    return G.astype(BF)


ENG = ("t", "v", "a", "p")
DMAS = ("c", "m", "cf", "mk", "fo0", "fo1", "fw0", "fw1", "wf0", "wf1", "ga", "gb", "out",
        "x0", "x1", "x2", "x3", "wq0", "wq1", "wb0", "wb1", "db")
ALLSEMS = ENG + DMAS


def build_program():
    META.clear()
    nc = bass.Bass("TRN2", target_bir_lowering=False)

    xT = nc.dram_tensor("xT", [D, NTOK], F32, kind="ExternalInput")
    wqd = nc.dram_tensor("wqd", [HKV, 128, KC, 128], F32, kind="ExternalInput")
    bqd = nc.dram_tensor("bqd", [128, HKV], F32, kind="ExternalInput")
    mhd = nc.dram_tensor("mhd", [128, HKV, 128], F32, kind="ExternalInput")
    zmd = nc.dram_tensor("zmd", [128, HKV, HKV], F32, kind="ExternalInput")
    wod = nc.dram_tensor("wod", [D, D], F32, kind="ExternalInput")
    wbd = nc.dram_tensor("wbd", [KC, 128, KC, 128], F32, kind="ExternalInput")
    bbd = nc.dram_tensor("bbd", [128, KC], F32, kind="ExternalInput")
    encd = nc.dram_tensor("encd", [128, KC, 40], F32, kind="ExternalInput")
    cfd = nc.dram_tensor("cfd", [128, KC, 3, 40], F32, kind="ExternalInput")
    mkd = nc.dram_tensor("mkd", [128, KC, 3, 40], U8, kind="ExternalInput")
    gtd = nc.dram_tensor("gtd", [D, D], BF16, kind="ExternalInput")
    seld = nc.dram_tensor("seld", [HKV, HKV, 128], BF16, kind="ExternalInput")
    onescd = nc.dram_tensor("onescd", [128, 1], BF16, kind="ExternalInput")
    onesrd = nc.dram_tensor("onesrd", [1, 128], BF16, kind="ExternalInput")
    epslnd = nc.dram_tensor("epslnd", [1, 1], F32, kind="ExternalInput")
    epsdend = nc.dram_tensor("epsdend", [HKV, 1], F32, kind="ExternalInput")
    oT = nc.dram_tensor("oT", [D, NTOK], F32, kind="ExternalOutput")
    wofs = nc.dram_tensor("wofs", [KC, 128, HKV, 128], BF16, kind="Internal")
    if DBG:
        dWc = nc.dram_tensor("dWc", [128, KC * 40], BF16, kind="ExternalOutput")
        dAt = nc.dram_tensor("dAt", [40, D], BF16, kind="ExternalOutput")
        dEbf = nc.dram_tensor("dEbf", [40, TT], BF16, kind="ExternalOutput")
        dDen = nc.dram_tensor("dDen", [HKV, TT], F32, kind="ExternalOutput")
        dU = nc.dram_tensor("dU", [128, KC * TT], BF16, kind="ExternalOutput")
        dRetr = nc.dram_tensor("dRetr", [128, HKV * TT], BF16, kind="ExternalOutput")
        dY = nc.dram_tensor("dY", [128, KC * TT], BF16, kind="ExternalOutput")
        dF = nc.dram_tensor("dF", [128, KC * TT], BF16, kind="ExternalOutput")
        dV = nc.dram_tensor("dV", [128, KC * TT], BF16, kind="ExternalOutput")
        dSq = nc.dram_tensor("dSq", [HKV, 128, TT], BF16, kind="ExternalOutput")
        dWofs = nc.dram_tensor("dWofs", [KC, 128, HKV, 128], BF16, kind="ExternalOutput")
    DBGG = {"done": 0}
    dump_u = [0] * KC
    dump_y = [0] * KC

    A = nc.alloc_sbuf_tensor
    xbuf = A("xbuf", [128, KC * TT], BF16)
    freg = A("freg", [128, KC * TT], BF16)
    wb_sb = [A(f"wb{i}", [128, KC * 128], BF16) for i in range(2)]
    usq = [A(f"usq{i}", [128, TT], BF16) for i in range(2)]
    outf = [A(f"outf{i}", [128, TT], F32) for i in range(2)]
    m_bf = A("m_bf", [128, HKV * 128], BF16)
    zm_bf = A("zm_bf", [128, HKV * HKV], BF16)
    wcat_f = A("wcat_f", [128, KC * 40], F32)
    wcat_bf = A("wcat_bf", [128, KC * 40], BF16)
    enc_bf = A("enc_bf", [128, KC * 40], BF16)
    at_half = A("at_half", [40, D], BF16)
    sel_sb = A("sel_sb", [HKV, HKV * 128], BF16)
    e_bf = A("e_bf", [40, TT], BF16)
    den_sb = A("den_sb", [HKV, TT], F32)
    rden_bf = A("rden_bf", [HKV, TT], BF16)
    rows = A("rows", [1, 8 * 512], F32)
    rowsb = A("rowsb", [1, 4 * 512], BF16)
    bq_sb = A("bq_sb", [128, HKV], F32)
    bb_sb = A("bb_sb", [128, KC], F32)
    onesc = A("onesc", [128, 1], BF16)
    onesr = A("onesr", [1, 128], BF16)
    epsln = A("epsln", [1, 1], F32)
    epsden = A("epsden", [HKV, 1], F32)

    freg_addr = nc.lookup_mloc(freg).addr
    Aat = nc.alloc_sbuf_tensor_at
    retr = Aat("retr", [128, HKV * TT], BF16, offset=freg_addr + 0)
    wq_sb = [Aat(f"wqs{i}", [128, KC * 128], BF16, offset=freg_addr + 16384 + i * 8192)
             for i in range(2)]
    wofm = [Aat(f"wofm{i}", [128, HKV * 128], BF16, offset=freg_addr + 32768 + i * 2048)
            for i in range(2)]
    sq_sb = [Aat(f"sqs{i}", [128, TT], BF16, offset=freg_addr + 36864 + i * 2048)
             for i in range(2)]
    cst = [Aat(f"cst{i}", [128, 8 * 120], F32, offset=freg_addr + 40960 + i * 3840)
           for i in range(2)]
    mst = [Aat(f"mst{i}", [128, 8 * 120], U8, offset=freg_addr + 48640 + i * 960)
           for i in range(2)]
    fbuf = [[Aat(f"fb{s}_{i}", [128, 1024], BF16,
                 offset=freg_addr + 40960 + (s * 4 + i) * 2048)
             for i in range(4)] for s in range(2)]
    mkf = Aat("mkf", [128, 8 * 120], F32, offset=freg_addr + 58752)
    wtmp = Aat("wtmp", [128, 320], F32, offset=freg_addr + 62592)
    usq0a = nc.lookup_mloc(usq[0]).addr
    usq1a = nc.lookup_mloc(usq[1]).addr
    mscr = [Aat("mscr0", [128, 512], F32, offset=usq0a),
            Aat("mscr1", [128, 512], F32, offset=usq1a)]
    rb_sb = A("rb_sb", [128, 512], BF16)

    ab1_sb = Aat("ab1_sb", [128, 2 * TT], BF16, offset=nc.lookup_mloc(wcat_f).addr)

    P = [nc.alloc_psum_tensor(f"P{i}", [128, TT], F32) for i in range(4)]

    QK = ["t", "v", "a", "s", "g"]
    Q = {k: [] for k in QK}
    CNT = {k: 0 for k in ALLSEMS}

    def emit(eng, body, waits=(), inc=False, tag=""):
        isem = "p" if eng == "g" else eng
        if inc:
            assert isem in ENG
            CNT[isem] += 1
        filtered = tuple((wk, wv) for (wk, wv) in waits
                         if isinstance(wv, dict) or wv > 0)
        META.append((eng, filtered, isem if inc else None, 1, tag))

        def f(E, SEMS, body=body, filtered=filtered, inc=inc, isem=isem):
            for (wk, wv) in filtered:
                v = wv["v"] if isinstance(wv, dict) else wv
                if v > 0:
                    E.wait_ge(SEMS[wk], v)
            ins = body(E)
            if inc:
                ins.then_inc(SEMS[isem], 1)
        Q[eng].append(f)
        return CNT[isem] if isem in ENG else None

    def dma(qeng, semk, dst, src, waits=()):
        CNT[semk] += 16
        filtered = tuple((wk, wv) for (wk, wv) in waits
                         if isinstance(wv, dict) or wv > 0)
        META.append((qeng, filtered, semk, 16, "dma"))

        def f(E, SEMS, dst=dst, src=src, filtered=filtered, semk=semk):
            for (wk, wv) in filtered:
                v = wv["v"] if isinstance(wv, dict) else wv
                if v > 0:
                    E.wait_ge(SEMS[wk], v)
            E.dma_start(dst, src).then_inc(SEMS[semk], 16)
        Q[qeng].append(f)
        return CNT[semk]

    def barrier(bit):
        if not (SERIAL & bit):
            return
        snap = dict(CNT)
        for e in QK:
            emit(e, lambda E: E.nop(),
                 [(k2, snap[k2]) for k2 in ALLSEMS if k2 != e], inc=(e in ENG))

    dma("s", "c", bq_sb[:], bqd[:])
    dma("s", "c", bb_sb[:], bbd[:])
    dma("s", "c", sel_sb[:], seld.rearrange("r h p -> r (h p)"))
    dma("s", "c", onesc[:], onescd[:])
    dma("s", "c", onesr[:], onesrd[:])
    dma("s", "c", epsln[:], epslnd[:])
    C_TOT = dma("s", "c", epsden[:], epsdend[:])
    dma("g", "m", m_bf[:], mhd.rearrange("p h e -> p (h e)"))
    dma("g", "m", zm_bf[:], zmd.rearrange("p h r -> p (h r)"))
    M_TOT = dma("g", "m", enc_bf[:], encd.rearrange("p k r -> p (k r)"))

    cslice = lambda t_, e: t_.rearrange("p (k e r) -> p k e r", k=8, e=3, r=40)[:, :, e, :]
    mslab3 = mkf.rearrange("p (k e r) -> p k e r", k=8, e=3, r=40)
    wtmp3 = wtmp.rearrange("p (k r) -> p k r", k=8, r=40)
    v_wg = [0] * 4
    for g in range(4):
        pg = g % 2
        w_war = [("v", v_wg[g - 1])] if g >= 1 else []
        c_cf = dma("s", "cf", cst[pg][:],
                   cfd[:, g * 8:(g + 1) * 8].rearrange("p k e r -> p (k e r)"),
                   waits=w_war)
        c_mk = dma("s", "mk", mst[pg][:],
                   mkd[:, g * 8:(g + 1) * 8].rearrange("p k e r -> p (k e r)"),
                   waits=w_war)
        wslab = wcat_f.rearrange("p (k r) -> p k r", k=KC, r=40)[:, g * 8:(g + 1) * 8, :]
        vc = emit("v", lambda E, pg=pg: E.tensor_copy(mkf[:], mst[pg][:]),
                  [("cf", c_cf), ("mk", c_mk)], inc=True)
        vc = emit("v", lambda E, pg=pg, wslab=wslab: E.tensor_tensor(
            wslab, cslice(cst[pg], 0), mslab3[:, :, 0, :], ALU.mult),
            [("v", vc)], inc=True)
        for e in (1, 2):
            vc = emit("v", lambda E, pg=pg, e=e: E.tensor_tensor(
                wtmp3, cslice(cst[pg], e), mslab3[:, :, e, :], ALU.mult),
                [("v", vc)], inc=True)
            vc = v_wg[g] = emit("v", lambda E, wslab=wslab: E.tensor_tensor(
                wslab, wslab, wtmp3, ALU.add), [("v", vc)], inc=True)
    v_wcat = emit("v", lambda E: E.tensor_copy(wcat_bf[:], wcat_f[:]),
                  [("v", v_wg[3])], inc=True)
    if DBG:
        dma("s", "db", dWc[:], wcat_bf[:], waits=[("v", v_wcat)])

    class Fold:
        def __init__(self):
            self.load_cnt = [0] * 32
            self.add_cell = [{"v": 0} for _ in range(32)]
            self.write_cell = [{"v": 0} for _ in range(32)]

        def emit_loads(self, p):
            cq, kv = p // 8, p % 8
            s = p % 2
            semk = f"fo{s}"
            cols = slice(cq * 1024, (cq + 1) * 1024)
            for gg in range(4):
                if p < 2:
                    w = [("v", v_wcat)]
                elif gg == 0:
                    w = [(f"fw{p % 2}", self.write_cell[p - 2])]
                else:
                    w = [("p", self.add_cell[p - 2])] if gg == 1 else []
                self.load_cnt[p] = dma("g", semk, fbuf[s][gg][:],
                                       wod[(kv * 4 + gg) * 128:(kv * 4 + gg + 1) * 128,
                                           cols],
                                       waits=w)

        def emit_adds_write(self, p):
            cq, kv = p // 8, p % 8
            s = p % 2
            semk = f"fo{s}"
            vv = emit("g", lambda E, s=s: E.tensor_add(
                fbuf[s][0][:], fbuf[s][0][:], fbuf[s][1][:]),
                [(semk, self.load_cnt[p])], inc=True)
            vv = emit("g", lambda E, s=s: E.tensor_add(
                fbuf[s][0][:], fbuf[s][0][:], fbuf[s][2][:]), [("p", vv)], inc=True)
            vv = emit("g", lambda E, s=s: E.tensor_add(
                fbuf[s][0][:], fbuf[s][0][:], fbuf[s][3][:]), [("p", vv)], inc=True)
            self.add_cell[p]["v"] = vv
            self.write_cell[p]["v"] = dma(
                "s", f"fw{s}",
                wofs[cq * 8:(cq + 1) * 8, :, kv, :].rearrange("m p c -> p m c"),
                fbuf[s][0].rearrange("p (m c) -> p m c", m=8, c=128),
                waits=[("p", vv)])

    fold = Fold()


    out_tt_done = [0] * KC
    outdma_done = [0] * KC
    v_done_all = 0
    q_done = {}
    f_done = {}
    fev_done = {}
    aprep_T = 0
    aprep_A = 0
    prev_sq_gate = [0] * KC
    prev_out_tt_all = 0

    for t in range(NTILES):
        tok0 = t * TT
        barrier(1024)

        xg_tot = [0] * 4
        wq_dma = {}
        for k in range(KC):
            w = [("v", out_tt_done[k])] if t > 0 else []
            if DBG and t > 0:
                w.append(("db", DBGG["done"]))
            xg_tot[k // 8] = dma("g", f"x{k // 8}", xbuf[:, k * TT:(k + 1) * TT],
                                 xT[k * 128:(k + 1) * 128, tok0:tok0 + TT], waits=w)
            if k % 4 == 3 and k // 4 < 2:
                h = k // 4
                wqw = []
                if t > 0:
                    wqw.append(("t", q_done.get((t - 1, h + HKV - 2), 0)))
                    wqw.append(("v", v_done_all))
                    if DBG:
                        wqw.append(("db", DBGG["done"]))
                wq_dma[h] = dma("g", f"wq{h % 2}", wq_sb[h % 2][:],
                                wqd[h].rearrange("p k o -> p (k o)"), waits=wqw)

        wqcell = {}
        if t == 0:
            WQPOS = {0: 2, 2: 3, 4: 4, 6: 5, 8: 6, 10: 7}
            for p in range(32):
                fold.emit_loads(p)
                if p >= 1:
                    fold.emit_adds_write(p - 1)
                if p in WQPOS:
                    h = WQPOS[p]
                    wqcell[h] = {"v": 0}
                    wq_dma[h] = dma("g", f"wq{h % 2}", wq_sb[h % 2][:],
                                    wqd[h].rearrange("p k o -> p (k o)"),
                                    waits=[("t", wqcell[h])])
            fold.emit_adds_write(31)

        dump_sq = {}
        elu_done = {}
        num_T = {}
        den_T = 0
        numev_A = {}

        def emit_q(h):
            Pq = P[h % 2]
            last = 0
            for k in range(KC):
                wts = [(f"x{k // 8}", xg_tot[k // 8])] if k % 8 == 0 else []
                if k == 0:
                    wts.append((f"wq{h % 2}", wq_dma[h]))
                    if h >= 2:
                        wts.append(("v", elu_done[(h - 2, 1)]))
                    elif t > 0:
                        wts.append(("a", fev_done[(t - 1, 30 + (h % 2))]))
                for nt in range(NT2):
                    last = emit("t", lambda E, Pq=Pq, h=h, k=k, nt=nt: E.matmul(
                        Pq[:, nt * 512:(nt + 1) * 512],
                        wq_sb[h % 2][:, k * 128:(k + 1) * 128],
                        xbuf[:, k * TT + nt * 512: k * TT + (nt + 1) * 512],
                        start=(k == 0), stop=(k == KC - 1)),
                        waits=wts if nt == 0 else (), inc=(k == KC - 1))
            q_done[(t, h)] = last

        def emit_wq_prefetch(h):
            if t == 0:
                wqcell[h]["v"] = q_done[(t, h - 2)]
            else:
                wq_dma[h] = dma("g", f"wq{h % 2}", wq_sb[h % 2][:],
                                wqd[h].rearrange("p k o -> p (k o)"),
                                waits=[("t", q_done[(t, h - 2)])])

        def emit_elu(h):
            Pq = P[h % 2]
            for nt in range(NT2):
                wts = [("t", q_done[(t, h)])]
                if h == 0 and nt == 0:
                    wts.append(("c", C_TOT))
                    wts.append(("t", prev_sq_gate[KC - 1]))
                if h >= 2 and nt == 0:
                    wts.append(("t", den_T))
                if SERIAL & 8192 and h >= 1 and nt == 0:
                    wts.append(("a", numev_A.get(h - 1, 0)))
                emit("v", lambda E, Pq=Pq, h=h, nt=nt: E.tensor_scalar(
                    mscr[nt][:], Pq[:, nt * 512:(nt + 1) * 512],
                    bq_sb[:, h:h + 1], 0.0, ALU.add, ALU.min), wts, inc=True)
                a_rb = emit("a", lambda E, Pq=Pq, h=h, nt=nt: E.activation(
                    rb_sb[:], Pq[:, nt * 512:(nt + 1) * 512], AF.Relu,
                    bias=bq_sb[:, h:h + 1]),
                    [("t", q_done[(t, h)]), ("v", CNT["v"])], inc=True)
                vaw = [("v", CNT["v"])]
                if DBG and t == 0 and h >= 2 and nt == 0:
                    vaw.append(("db", dump_sq[h - 2]))
                if DBG and t == 1 and h == 0 and nt == 0:
                    vaw.append(("db", DBGG["done"]))
                va = emit("a", lambda E, h=h, nt=nt: E.activation(
                    sq_sb[h % 2][:, nt * 512:(nt + 1) * 512], mscr[nt][:], AF.Exp),
                    vaw, inc=True)
                elu_done[(h, nt)] = emit("v", lambda E, h=h, nt=nt: E.tensor_tensor(
                    sq_sb[h % 2][:, nt * 512:(nt + 1) * 512],
                    sq_sb[h % 2][:, nt * 512:(nt + 1) * 512],
                    rb_sb[:], ALU.add), [("a", va)], inc=True)
            if DBG and t == 0:
                dump_sq[h] = dma("s", "db", dSq[h], sq_sb[h % 2][:],
                                 waits=[("v", elu_done[(h, 1)])])

        def emit_numden(h):
            nonlocal den_T
            for nt in range(NT2):
                wts = [("v", elu_done[(h, nt)])]
                if nt == 0:
                    if h > 0:
                        wts.append(("a", numev_A[h - 1]))
                    elif t > 0:
                        wts.append(("v", prev_out_tt_all))
                    if h == 0 and t == 0:
                        wts.append(("m", M_TOT))
                num_T[(h, nt)] = emit("t", lambda E, h=h, nt=nt: E.matmul(
                    P[2][:, nt * 512:(nt + 1) * 512],
                    m_bf[:, h * 128:(h + 1) * 128],
                    sq_sb[h % 2][:, nt * 512:(nt + 1) * 512],
                    start=True, stop=True), wts, inc=True)
            for nt in range(NT2):
                wts = []
                if h == 0 and nt == 0:
                    if t == 0:
                        wts.append(("m", M_TOT))
                    if t > 0:
                        wts.append(("v", prev_out_tt_all))
                den_T = emit("t", lambda E, h=h, nt=nt: E.matmul(
                    P[3][0:HKV, nt * 512:(nt + 1) * 512],
                    zm_bf[:, h * HKV:(h + 1) * HKV],
                    sq_sb[h % 2][:, nt * 512:(nt + 1) * 512],
                    start=(h == 0), stop=(h == HKV - 1)), wts, inc=True)
            for nt in range(NT2):
                nev = [("t", num_T[(h, nt)])]
                if DBG and t == 1 and h == 0 and nt == 0:
                    nev.append(("db", DBGG["done"]))
                numev_A[h] = emit("a", lambda E, h=h, nt=nt: E.activation(
                    retr[:, h * TT + nt * 512: h * TT + (nt + 1) * 512],
                    P[2][:, nt * 512:(nt + 1) * 512], AF.Copy),
                    nev, inc=True)

        if SERIAL & 8192:
            for h in range(HKV):
                emit_q(h)
                if h + 2 < HKV:
                    emit_wq_prefetch(h + 2)
                emit_elu(h)
                if t == 0:
                    for p in range(4 * h, min(4 * h + 4, 32)):
                        fold.emit_adds_write(p)
                emit_numden(h)

        else:
            emit_q(0)
            emit_wq_prefetch(2)
            emit_elu(0)
            for h in range(1, HKV):
                emit_q(h)
                if h + 2 < HKV:
                    emit_wq_prefetch(h + 2)
                emit_elu(h)
                emit_numden(h - 1)
            emit_numden(HKV - 1)

        barrier(1)

        if DBG and t == 0:
            dma("s", "db", dWofs[:], wofs[:], waits=[("fw0", 256), ("fw1", 256)])
        a_den = emit("a", lambda E: E.activation(
            den_sb[:], P[3][0:HKV, :], AF.Copy, bias=float(EPS_DEN)),
            [("t", den_T)], inc=True)
        dden_c = 0
        if DBG and t == 0:
            dden_c = dma("s", "db", dDen[:], den_sb[:], waits=[("a", a_den)])
        v_rec = emit("v", lambda E: E.reciprocal(den_sb[:], den_sb[:]),
                     [("a", a_den)] + ([("db", dden_c)] if (DBG and t == 0) else []),
                     inc=True)
        a_rden = emit("a", lambda E: E.activation(rden_bf[:], den_sb[:], AF.Copy),
                      [("v", v_rec)], inc=True)
        retr_V = 0
        for h in range(HKV):
            bc_T = {}
            for nt in range(NT2):
                wts = [("a", a_rden), ("c", C_TOT)]
                if h > 0:
                    wts.append(("v", retr_V))
                bc_T[nt] = emit("t", lambda E, h=h, nt=nt: E.matmul(
                    P[3][:, nt * 512:(nt + 1) * 512],
                    sel_sb[:, h * 128:(h + 1) * 128],
                    rden_bf[0:HKV, nt * 512:(nt + 1) * 512],
                    start=True, stop=True), wts, inc=True)
            for nt in range(NT2):
                retr_V = emit("v", lambda E, h=h, nt=nt: E.tensor_tensor(
                    retr[:, h * TT + nt * 512: h * TT + (nt + 1) * 512],
                    retr[:, h * TT + nt * 512: h * TT + (nt + 1) * 512],
                    P[3][:, nt * 512:(nt + 1) * 512], ALU.mult),
                    [("t", bc_T[nt])], inc=True)
        barrier(2)
        if DBG and t == 0:
            dma("s", "db", dRetr[:], retr[:], waits=[("v", retr_V)])

        wofm_dma = {}
        wof_T = {}
        u_done = [0] * KC
        for m in range(2):
            wts = []
            if t == 0:
                wts += [("fw0", 16 * 16), ("fw1", 16 * 16)]
            if t > 0:
                wts.append(("v", v_done_all))
                if DBG:
                    wts.append(("db", DBGG["done"]))
            wofm_dma[m] = dma("s", f"wf{m % 2}", wofm[m % 2][:],
                              wofs[m].rearrange("p kv c -> p (kv c)"), waits=wts)
        for m in range(KC):
            last = 0
            for kv in range(HKV):
                wts = []
                if kv == 0:
                    wts = [(f"wf{m % 2}", wofm_dma[m]), ("v", retr_V)]
                    if m >= 2:
                        wts.append(("v", u_done[m - 2]))
                    elif t > 0:
                        wts.append(("a", fev_done[(t - 1, 30 + (m % 2))]))
                for nt in range(NT2):
                    last = emit("t", lambda E, m=m, kv=kv, nt=nt: E.matmul(
                        P[m % 2][:, nt * 512:(nt + 1) * 512],
                        wofm[m % 2][:, kv * 128:(kv + 1) * 128],
                        retr[:, kv * TT + nt * 512: kv * TT + (nt + 1) * 512],
                        start=(kv == 0), stop=(kv == HKV - 1)),
                        waits=wts if nt == 0 else (), inc=(kv == HKV - 1))
            wof_T[m] = last
            if m + 2 < KC:
                wofm_dma[m + 2] = dma("s", f"wf{m % 2}", wofm[m % 2][:],
                                      wofs[m + 2].rearrange("p kv c -> p (kv c)"),
                                      waits=[("t", wof_T[m])])
            u_done[m] = emit("v", lambda E, m=m: E.tensor_tensor(
                xbuf[:, m * TT:(m + 1) * TT], xbuf[:, m * TT:(m + 1) * TT],
                P[m % 2][:], ALU.add), [("t", wof_T[m])], inc=True)
            if DBG and t == 0:
                dump_u[m] = dma("s", "db", dU[:, m * TT:(m + 1) * TT],
                                xbuf[:, m * TT:(m + 1) * TT],
                                waits=[("v", u_done[m])])
        barrier(4)

        if t == 0:
            apmm_k = [0] * KC
            G_dma = [0] * KC
            G_dma[0] = dma("g", "ga", wb_sb[0][:], gtd[0:128, :])
            G_dma[1] = dma("g", "gb", wb_sb[1][:], gtd[128:256, :])
            for k in range(KC):
                wts = [(("ga" if k % 2 == 0 else "gb"), G_dma[k])]
                if k == 0:
                    wts += [("v", max(u_done)), ("v", v_wcat)]
                for jp in range(8):
                    apmm_k[k] = emit("t", lambda E, k=k, jp=jp: E.matmul(
                        P[jp // 2][0:40, (jp % 2) * 512:(jp % 2 + 1) * 512],
                        wcat_bf[:, k * 40:(k + 1) * 40],
                        wb_sb[k % 2][:, jp * 512:(jp + 1) * 512],
                        start=(k == 0), stop=(k == KC - 1)),
                        waits=wts if jp == 0 else (), inc=True)
                if k + 2 < KC:
                    G_dma[k + 2] = dma("g", ("ga" if k % 2 == 0 else "gb"),
                                       wb_sb[k % 2][:],
                                       gtd[(k + 2) * 128:(k + 3) * 128, :],
                                       waits=[("t", apmm_k[k])])
            aprep_T = apmm_k[KC - 1]
            for jp in range(8):
                aprep_A = emit("a", lambda E, jp=jp: E.activation(
                    at_half[:, jp * 512:(jp + 1) * 512],
                    P[jp // 2][0:40, (jp % 2) * 512:(jp % 2 + 1) * 512],
                    AF.Copy, scale=0.5), [("t", aprep_T)], inc=True)
            if DBG:
                dma("s", "db", dAt[:], at_half[:], waits=[("a", aprep_A)])
        barrier(8)

        elu_last_v = elu_done[(HKV - 1, 1)]

        def layernorm(src_done, first_gate_A, usq_prev_v=0):
            mean_T = [0, 0]
            sq_T = [0, 0]
            sqmm_k = [0] * KC
            usq_A = [0] * KC
            for k in range(KC):
                wts_u = [("v", src_done[k])]
                if k == 0:
                    wts_u.append(("c", C_TOT))
                    wts_u.append(("a", first_gate_A))
                for nt in range(NT2):
                    mean_T[nt] = emit("t", lambda E, k=k, nt=nt: E.matmul(
                        P[2][0:1, nt * 512:(nt + 1) * 512], onesc[:],
                        xbuf[:, k * TT + nt * 512: k * TT + (nt + 1) * 512],
                        start=(k == 0), stop=(k == KC - 1)),
                        waits=wts_u if nt == 0 else (), inc=(k == KC - 1))
                wsq = [("v", src_done[k]),
                       ("t", prev_sq_gate[KC - 2 + k] if k < 2 else sqmm_k[k - 2])]
                if k < 2 and usq_prev_v:
                    wsq.append(("v", usq_prev_v))
                usq_A[k] = emit("a", lambda E, k=k: E.activation(
                    usq[k % 2][:], xbuf[:, k * TT:(k + 1) * TT], AF.Square),
                    wsq, inc=True)
                for nt in range(NT2):
                    sq_T[nt] = emit("t", lambda E, k=k, nt=nt: E.matmul(
                        P[3][0:1, nt * 512:(nt + 1) * 512], onesc[:],
                        usq[k % 2][:, nt * 512:(nt + 1) * 512],
                        start=(k == 0), stop=(k == KC - 1)),
                        waits=[("a", usq_A[k])] if nt == 0 else (),
                        inc=(nt == NT2 - 1 or k == KC - 1))
                sqmm_k[k] = sq_T[NT2 - 1]
            R = lambda i: rows[0:1, i * 512:(i + 1) * 512]
            a_mean = [0, 0]
            for nt in range(NT2):
                a_mean[nt] = emit("a", lambda E, nt=nt: E.activation(
                    R(0 + nt), P[2][0:1, nt * 512:(nt + 1) * 512], AF.Copy,
                    scale=1.0 / D), [("t", mean_T[nt])], inc=True)
                emit("a", lambda E, nt=nt: E.activation(
                    R(2 + nt), P[3][0:1, nt * 512:(nt + 1) * 512], AF.Copy,
                    scale=1.0 / D), [("t", sq_T[nt])], inc=True)
            a_msq = CNT["a"]
            v_mm = emit("v", lambda E: (
                E.tensor_mul(R(4), R(0), R(0)),
                E.tensor_mul(R(5), R(1), R(1)))[-1], [("a", a_msq)], inc=True)
            v_var = emit("v", lambda E: (
                E.tensor_sub(R(2), R(2), R(4)),
                E.tensor_sub(R(3), R(3), R(5)))[-1], [("v", v_mm)], inc=True)
            a_ln = emit("a", lambda E: (
                E.activation(R(4), R(2), AF.Ln, bias=epsln[:]),
                E.activation(R(5), R(3), AF.Ln, bias=epsln[:]))[-1],
                [("v", v_var), ("c", C_TOT)], inc=True)
            a_rstd = emit("a", lambda E: (
                E.activation(R(6), R(4), AF.Exp, scale=-0.5),
                E.activation(R(7), R(5), AF.Exp, scale=-0.5))[-1],
                [("a", a_ln)], inc=True)
            v_brow = emit("v", lambda E: (
                E.tensor_mul(R(4), R(0), R(6)),
                E.tensor_mul(R(5), R(1), R(7)))[-1], [("a", a_rstd)], inc=True)
            Rb = lambda i: rowsb[0:1, i * 512:(i + 1) * 512]
            a_rowsb = emit("a", lambda E: (
                E.activation(Rb(0), R(6), AF.Copy),
                E.activation(Rb(1), R(7), AF.Copy),
                E.activation(Rb(2), R(4), AF.Copy),
                E.activation(Rb(3), R(5), AF.Copy))[-1], [("v", v_brow)], inc=True)
            return a_rowsb, sqmm_k, max(a_mean)

        ln1_rowsb, ln1_sqmm, ln1_meanA = layernorm(
            u_done, aprep_A if t == 0 else 0, usq_prev_v=elu_last_v)
        prev_sq_gate = ln1_sqmm
        bc1_T = 0
        for nt in range(NT2):
            bc1_T = emit("t", lambda E, nt=nt: E.matmul(
                P[0][:, nt * 512:(nt + 1) * 512], onesr[:],
                rowsb[0:1, nt * 512:(nt + 1) * 512], start=True, stop=True),
                [("a", ln1_rowsb), ("c", C_TOT)] if nt == 0 else (), inc=True)
            bc1_T = emit("t", lambda E, nt=nt: E.matmul(
                P[1][:, nt * 512:(nt + 1) * 512], onesr[:],
                rowsb[0:1, (2 + nt) * 512:(3 + nt) * 512], start=True, stop=True),
                (), inc=True)
        a_ab1a = emit("a", lambda E: E.activation(ab1_sb[:, 0:TT], P[0][:], AF.Copy),
                      [("t", bc1_T)], inc=True)
        a_ab1b = emit("a", lambda E: E.activation(ab1_sb[:, TT:2 * TT], P[1][:], AF.Copy),
                      [("t", bc1_T)], inc=True)
        norm_done = [0] * KC
        for k in range(KC):
            nw = [("a", a_ab1b)] if k == 0 else []
            if DBG and t == 0:
                nw.append(("db", dump_u[k]))
            v_mul = emit("v", lambda E, k=k: E.tensor_tensor(
                xbuf[:, k * TT:(k + 1) * TT], xbuf[:, k * TT:(k + 1) * TT],
                ab1_sb[:, 0:TT], ALU.mult), nw, inc=True)
            norm_done[k] = emit("v", lambda E, k=k: E.tensor_tensor(
                xbuf[:, k * TT:(k + 1) * TT], xbuf[:, k * TT:(k + 1) * TT],
                ab1_sb[:, TT:2 * TT], ALU.subtract), [("v", v_mul)], inc=True)
            if DBG and t == 0:
                dump_y[k] = dma("s", "db", dY[:, k * TT:(k + 1) * TT],
                                xbuf[:, k * TT:(k + 1) * TT],
                                waits=[("v", norm_done[k])])
        barrier(16)

        e_T = 0
        for k in range(KC):
            wts = [("v", norm_done[k])]
            if k == 0:
                wts += [("m", M_TOT), ("a", ln1_meanA)]
            for nt in range(NT2):
                e_T = emit("t", lambda E, k=k, nt=nt: E.matmul(
                    P[2][0:40, nt * 512:(nt + 1) * 512],
                    enc_bf[:, k * 40:(k + 1) * 40],
                    xbuf[:, k * TT + nt * 512: k * TT + (nt + 1) * 512],
                    start=(k == 0), stop=(k == KC - 1)),
                    waits=wts if nt == 0 else (), inc=(k == KC - 1))
        ebf_A = emit("a", lambda E: E.activation(e_bf[:], P[2][0:40, :], AF.Copy),
                     [("t", e_T)], inc=True)
        if DBG and t == 0:
            dma("s", "db", dEbf[:], e_bf[:], waits=[("a", ebf_A)])
        barrier(32)
        wb_dma = {}

        def emit_wb(m, wts):
            semk = f"wb{m % 2}"
            flat = wbd[m].rearrange("p k o -> p (k o)")
            h0 = dma("g", semk, wb_sb[m % 2][:, 0:2048], flat[:, 0:2048], waits=wts)
            h1 = dma("g", semk, wb_sb[m % 2][:, 2048:4096], flat[:, 2048:4096])
            wb_dma[m] = (h0, h1)

        for m in range(2):
            wts = [("t", aprep_T)] if t == 0 else \
                [("t", f_done.get((t - 1, KC - 2 + m), 0))]
            emit_wb(m, wts)
        for m in range(KC):
            Pf = P[m % 2]
            last = 0
            for nt in range(NT2):
                wts = ()
                if nt == 0:
                    wts = [("a", ebf_A), (f"wb{m % 2}", wb_dma[m][0])]
                    if m < 2:
                        wts.append(("v", norm_done[KC - 1]))
                    else:
                        wts.append(("a", fev_done[(t, m - 2)]))
                    if SERIAL & 4096 and m >= 1:
                        wts.append(("a", fev_done[(t, m - 1)]))
                emit("t", lambda E, Pf=Pf, m=m, nt=nt: E.matmul(
                    Pf[:, nt * 512:(nt + 1) * 512],
                    at_half[:, m * 128:(m + 1) * 128],
                    e_bf[0:40, nt * 512:(nt + 1) * 512],
                    start=True, stop=False), waits=wts)
                for k in range(KC):
                    kw = [(f"wb{m % 2}", wb_dma[m][1])] if (nt == 0 and k == 16) else ()
                    last = emit("t", lambda E, Pf=Pf, m=m, k=k, nt=nt: E.matmul(
                        Pf[:, nt * 512:(nt + 1) * 512],
                        wb_sb[m % 2][:, k * 128:(k + 1) * 128],
                        xbuf[:, k * TT + nt * 512: k * TT + (nt + 1) * 512],
                        start=False, stop=(k == KC - 1)),
                        waits=kw, inc=(k == KC - 1))
            f_done[(t, m)] = last
            fev_done[(t, m)] = emit("a", lambda E, m=m: E.activation(
                freg[:, m * TT:(m + 1) * TT], P[m % 2][:], AF.Copy),
                [("t", f_done[(t, m)])], inc=True)
            if m + 2 < KC:
                wbw = [("t", f_done[(t, m)])]
                if SERIAL & 4096:
                    wbw.append(("a", fev_done[(t, m)]))
                emit_wb(m + 2, wbw)
        barrier(64)
        vv_done = [0] * KC
        for m in range(KC):
            vw = [("a", fev_done[(t, m)]), ("t", f_done[(t, KC - 1)])]
            if m == 0:
                vw.append(("c", C_TOT))
            if DBG and t == 0:
                vw.append(("db", dump_y[m]))
                dma("s", "db", dF[:, m * TT:(m + 1) * TT], freg[:, m * TT:(m + 1) * TT],
                    waits=[("a", fev_done[(t, m)])])
            vv_done[m] = emit("v", lambda E, m=m: E.scalar_tensor_tensor(
                xbuf[:, m * TT:(m + 1) * TT], freg[:, m * TT:(m + 1) * TT],
                bb_sb[:, m:m + 1], xbuf[:, m * TT:(m + 1) * TT],
                ALU.add, ALU.add), vw, inc=True)
            if DBG and t == 0:
                dma("s", "db", dV[:, m * TT:(m + 1) * TT], xbuf[:, m * TT:(m + 1) * TT],
                    waits=[("v", vv_done[m])])
        v_done_all = vv_done[KC - 1]
        barrier(128)

        ln2_rowsb, ln2_sqmm, _ = layernorm(vv_done, ebf_A)
        prev_sq_gate = ln2_sqmm
        barrier(256)
        bc2_T = 0
        for nt in range(NT2):
            bc2_T = emit("t", lambda E, nt=nt: E.matmul(
                P[2][:, nt * 512:(nt + 1) * 512], onesr[:],
                rowsb[0:1, nt * 512:(nt + 1) * 512], start=True, stop=True),
                [("a", ln2_rowsb)] if nt == 0 else (), inc=True)
            bc2_T = emit("t", lambda E, nt=nt: E.matmul(
                P[3][:, nt * 512:(nt + 1) * 512], onesr[:],
                rowsb[0:1, (2 + nt) * 512:(3 + nt) * 512], start=True, stop=True),
                (), inc=True)
        new_outdma = [0] * KC
        for k in range(KC):
            wts = [("t", bc2_T)] if k == 0 else []
            ping_gate = CNT["out"] if (k >= 2 or t > 0) else 0
            if ping_gate:
                wts.append(("out", ping_gate))
            v_mul2 = emit("v", lambda E, k=k: E.tensor_tensor(
                outf[k % 2][:], xbuf[:, k * TT:(k + 1) * TT], P[2][:], ALU.mult),
                wts, inc=True)
            out_tt_done[k] = emit("v", lambda E, k=k: E.tensor_tensor(
                outf[k % 2][:], outf[k % 2][:], P[3][:], ALU.subtract),
                [("v", v_mul2)], inc=True)
            new_outdma[k] = dma("s", "out", oT[k * 128:(k + 1) * 128, tok0:tok0 + TT],
                                outf[k % 2][:], waits=[("v", out_tt_done[k])])
        barrier(512)
        outdma_done = new_outdma
        prev_out_tt_all = out_tt_done[KC - 1]
        if DBG and t == 0:
            DBGG["done"] = CNT["db"]

    emit("s", lambda E: E.nop(), [(k, CNT[k]) for k in DMAS if CNT[k] > 0])

    with ExitStack() as ctx:
        sems = {k: ctx.enter_context(nc.semaphore(f"S_{k}")) for k in ALLSEMS}
        block = ctx.enter_context(nc.Block(no_gpsimd_drain=True))
        block.tensor(lambda E: [f(E, sems) for f in Q["t"]])
        block.vector(lambda E: [f(E, sems) for f in Q["v"]])
        block.scalar(lambda E: [f(E, sems) for f in Q["a"]])
        block.sync(lambda E: [f(E, sems) for f in Q["s"]])
        block.gpsimd(lambda E: [f(E, sems) for f in Q["g"]])
    return nc


def _prep_inputs(inputs):
    _ = inputs.get("L0_attn_Wk"), inputs.get("L0_attn_bk")
    _ = inputs.get("L0_attn_Wv"), inputs.get("L0_attn_bv")
    f32 = np.float32
    x = np.ascontiguousarray(np.asarray(inputs["L0_attn_hidden_states"], f32).reshape(16384, D))
    Wq = np.asarray(inputs["L0_attn_Wq"], f32)
    bq = np.asarray(inputs["L0_attn_bq"], f32)
    Wo = np.asarray(inputs["L0_attn_Wo"], f32)
    M = np.asarray(inputs["L0_attn_M"], f32)
    z = np.asarray(inputs["L0_attn_z"], f32)
    Wb = np.asarray(inputs["L0_ffn_W_base"], f32)
    bb = np.asarray(inputs["L0_ffn_b_base"], f32)
    e0 = np.asarray(inputs["L0_ffn_enc0"], f32)
    e1 = np.asarray(inputs["L0_ffn_enc1"], f32)
    c0 = np.asarray(inputs["L0_ffn_coefs0"], f32)
    c1 = np.asarray(inputs["L0_ffn_coefs1"], f32)
    m0 = np.asarray(inputs["L0_ffn_mask0"])
    m1 = np.asarray(inputs["L0_ffn_mask1"])

    Wq_sel = Wq.reshape(32, 128, D)[::4].reshape(1024, D)
    wq_blk = np.ascontiguousarray(Wq_sel.T.reshape(KC, 128, HKV, 128).transpose(2, 1, 0, 3))
    bq_blk = np.ascontiguousarray(bq.reshape(32, 128)[::4].T)
    mh_blk = np.ascontiguousarray(M.transpose(1, 0, 2))
    zm_blk = np.zeros((128, HKV, HKV), f32)
    for h in range(HKV):
        zm_blk[:, h, h] = z[h]
    woT = np.ascontiguousarray(Wo.T)
    wb_blk = np.ascontiguousarray(Wb.T.reshape(KC, 128, KC, 128).transpose(2, 1, 0, 3))
    bb_blk = np.ascontiguousarray(bb.reshape(KC, 128).T)
    enc_cat = np.concatenate([e0, e1], axis=0)
    enc_blk = np.ascontiguousarray(enc_cat.T.reshape(KC, 128, 40).transpose(1, 0, 2))
    ccat = np.concatenate([c0.transpose(1, 0, 2), c1.transpose(1, 0, 2)], axis=2)
    cf_blk = np.ascontiguousarray(ccat.reshape(KC, 128, 3, 40).transpose(1, 0, 2, 3))
    mcat = np.concatenate([m0.transpose(1, 0, 2), m1.transpose(1, 0, 2)], axis=2)
    mk_blk = np.ascontiguousarray(mcat.reshape(KC, 128, 3, 40).transpose(1, 0, 2, 3)).view(np.uint8)

    sel_np = np.zeros((HKV, HKV, 128), f32)
    for h in range(HKV):
        sel_np[h, h, :] = 1.0
    shared = {
        "wqd": wq_blk, "bqd": bq_blk, "mhd": mh_blk, "zmd": zm_blk, "wod": woT,
        "wbd": wb_blk, "bbd": bb_blk, "encd": enc_blk, "cfd": cf_blk, "mkd": mk_blk,
        "gtd": np.ascontiguousarray(_build_G()),
        "seld": sel_np.astype(BF),
        "onescd": np.ones((128, 1), f32).astype(BF),
        "onesrd": np.ones((1, 128), f32).astype(BF),
        "epslnd": np.full((1, 1), EPS_LN, f32),
        "epsdend": np.full((HKV, 1), EPS_DEN, f32),
    }
    in_maps = []
    for c in range(NCORES):
        m = dict(shared)
        m["xT"] = np.ascontiguousarray(x[c * NTOK:(c + 1) * NTOK].T)
        in_maps.append(m)
    return in_maps


def kernel(**inputs):
    if "nc" not in _cache:
        _cache["nc"] = build_program()
    nc = _cache["nc"]
    in_maps = _prep_inputs(inputs)
    res = run_bass_kernel_spmd(nc, in_maps, core_ids=list(range(NCORES)))
    parts = [np.asarray(res.results[c]["oT"]).T for c in range(NCORES)]
    return np.concatenate(parts, axis=0).reshape(2, 8192, D).astype(np.float32)


# version 14
# speedup vs baseline: 1.0811x; 1.0028x over previous
import numpy as np
import ml_dtypes
import concourse.bass as bass
import concourse.mybir as mybir
from concourse.bass_utils import run_bass_kernel_spmd
from contextlib import ExitStack

BF = ml_dtypes.bfloat16
BF16 = mybir.dt.bfloat16
F32 = mybir.dt.float32
U8 = mybir.dt.uint8
AF = mybir.ActivationFunctionType
ALU = mybir.AluOpType

NCORES = 8
NTOK = 2048
TT = 1024
NTILES = NTOK // TT
D = 4096
KC = 32
HKV = 8
NT2 = TT // 512
EPS_LN = 1e-5
EPS_DEN = 1e-6

_cache = {}
META = []
DBG = False
SERIAL = 0


def _build_G():
    n = D
    j = np.arange(n)[None, :].astype(np.float64)
    G = np.zeros((n, n), dtype=np.float64)
    G[0, :] = 1.0 / np.sqrt(n)
    maxf = (n - 1) // 2
    f = np.arange(1, maxf + 1)[:, None].astype(np.float64)
    ang = 2.0 * np.pi * f * j / n
    G[1:1 + 2 * maxf:2, :] = np.sqrt(2.0 / n) * np.cos(ang)
    G[2:2 + 2 * maxf:2, :] = np.sqrt(2.0 / n) * np.sin(ang)
    G[n - 1, :] = ((-1.0) ** j[0]) / np.sqrt(n)
    return G.astype(BF)


ENG = ("t", "v", "a", "p")
DMAS = ("c", "m", "cf", "mk", "fo0", "fo1", "fw0", "fw1", "wf0", "wf1", "ga", "gb", "out",
        "x0", "x1", "x2", "x3", "wq0", "wq1", "wb0", "wb1", "db")
ALLSEMS = ENG + DMAS


def build_program():
    META.clear()
    nc = bass.Bass("TRN2", target_bir_lowering=False)

    xT = nc.dram_tensor("xT", [D, NTOK], F32, kind="ExternalInput")
    wqd = nc.dram_tensor("wqd", [HKV, 128, KC, 128], F32, kind="ExternalInput")
    bqd = nc.dram_tensor("bqd", [128, HKV], F32, kind="ExternalInput")
    mhd = nc.dram_tensor("mhd", [128, HKV, 128], F32, kind="ExternalInput")
    zmd = nc.dram_tensor("zmd", [128, HKV, HKV], F32, kind="ExternalInput")
    wod = nc.dram_tensor("wod", [D, D], F32, kind="ExternalInput")
    wbd = nc.dram_tensor("wbd", [KC, 128, KC, 128], F32, kind="ExternalInput")
    bbd = nc.dram_tensor("bbd", [128, KC], F32, kind="ExternalInput")
    encd = nc.dram_tensor("encd", [128, KC, 40], F32, kind="ExternalInput")
    cfd = nc.dram_tensor("cfd", [128, KC, 3, 40], F32, kind="ExternalInput")
    mkd = nc.dram_tensor("mkd", [128, KC, 3, 40], U8, kind="ExternalInput")
    gtd = nc.dram_tensor("gtd", [D, D], BF16, kind="ExternalInput")
    seld = nc.dram_tensor("seld", [HKV, HKV, 128], BF16, kind="ExternalInput")
    onescd = nc.dram_tensor("onescd", [128, 1], BF16, kind="ExternalInput")
    onesrd = nc.dram_tensor("onesrd", [1, 128], BF16, kind="ExternalInput")
    epslnd = nc.dram_tensor("epslnd", [1, 1], F32, kind="ExternalInput")
    epsdend = nc.dram_tensor("epsdend", [HKV, 1], F32, kind="ExternalInput")
    oT = nc.dram_tensor("oT", [D, NTOK], F32, kind="ExternalOutput")
    wofs = nc.dram_tensor("wofs", [KC, 128, HKV, 128], BF16, kind="Internal")
    if DBG:
        dWc = nc.dram_tensor("dWc", [128, KC * 40], BF16, kind="ExternalOutput")
        dAt = nc.dram_tensor("dAt", [40, D], BF16, kind="ExternalOutput")
        dEbf = nc.dram_tensor("dEbf", [40, TT], BF16, kind="ExternalOutput")
        dDen = nc.dram_tensor("dDen", [HKV, TT], F32, kind="ExternalOutput")
        dU = nc.dram_tensor("dU", [128, KC * TT], BF16, kind="ExternalOutput")
        dRetr = nc.dram_tensor("dRetr", [128, HKV * TT], BF16, kind="ExternalOutput")
        dY = nc.dram_tensor("dY", [128, KC * TT], BF16, kind="ExternalOutput")
        dF = nc.dram_tensor("dF", [128, KC * TT], BF16, kind="ExternalOutput")
        dV = nc.dram_tensor("dV", [128, KC * TT], BF16, kind="ExternalOutput")
        dSq = nc.dram_tensor("dSq", [HKV, 128, TT], BF16, kind="ExternalOutput")
        dWofs = nc.dram_tensor("dWofs", [KC, 128, HKV, 128], BF16, kind="ExternalOutput")
    DBGG = {"done": 0}
    dump_u = [0] * KC
    dump_y = [0] * KC

    A = nc.alloc_sbuf_tensor
    xbuf = A("xbuf", [128, KC * TT], BF16)
    freg = A("freg", [128, KC * TT], BF16)
    wb_sb = [A(f"wb{i}", [128, KC * 128], BF16) for i in range(2)]
    usq = [A(f"usq{i}", [128, TT], BF16) for i in range(2)]
    outf = [A(f"outf{i}", [128, TT], F32) for i in range(2)]
    m_bf = A("m_bf", [128, HKV * 128], BF16)
    zm_bf = A("zm_bf", [128, HKV * HKV], BF16)
    wcat_f = A("wcat_f", [128, KC * 40], F32)
    wcat_bf = A("wcat_bf", [128, KC * 40], BF16)
    enc_bf = A("enc_bf", [128, KC * 40], BF16)
    at_half = A("at_half", [40, D], BF16)
    sel_sb = A("sel_sb", [HKV, HKV * 128], BF16)
    e_bf = A("e_bf", [40, TT], BF16)
    den_sb = A("den_sb", [HKV, TT], F32)
    rden_bf = A("rden_bf", [HKV, TT], BF16)
    rows = A("rows", [1, 8 * 512], F32)
    rowsb = A("rowsb", [1, 4 * 512], BF16)
    bq_sb = A("bq_sb", [128, HKV], F32)
    bb_sb = A("bb_sb", [128, KC], F32)
    onesc = A("onesc", [128, 1], BF16)
    onesr = A("onesr", [1, 128], BF16)
    epsln = A("epsln", [1, 1], F32)
    epsden = A("epsden", [HKV, 1], F32)

    freg_addr = nc.lookup_mloc(freg).addr
    Aat = nc.alloc_sbuf_tensor_at
    retr = Aat("retr", [128, HKV * TT], BF16, offset=freg_addr + 0)
    wq_sb = [Aat(f"wqs{i}", [128, KC * 128], BF16, offset=freg_addr + 16384 + i * 8192)
             for i in range(2)]
    wofm = [Aat(f"wofm{i}", [128, HKV * 128], BF16, offset=freg_addr + 32768 + i * 2048)
            for i in range(2)]
    sq_sb = [Aat(f"sqs{i}", [128, TT], BF16, offset=freg_addr + 36864 + i * 2048)
             for i in range(2)]
    cst = [Aat(f"cst{i}", [128, 8 * 120], F32, offset=freg_addr + 40960 + i * 3840)
           for i in range(2)]
    mst = [Aat(f"mst{i}", [128, 8 * 120], U8, offset=freg_addr + 48640 + i * 960)
           for i in range(2)]
    fbuf = [[Aat(f"fb{s}_{i}", [128, 1024], BF16,
                 offset=freg_addr + 40960 + (s * 4 + i) * 2048)
             for i in range(4)] for s in range(2)]
    mkf = Aat("mkf", [128, 8 * 120], F32, offset=freg_addr + 58752)
    wtmp = Aat("wtmp", [128, 320], F32, offset=freg_addr + 62592)
    usq0a = nc.lookup_mloc(usq[0]).addr
    usq1a = nc.lookup_mloc(usq[1]).addr
    mscr = [Aat("mscr0", [128, 512], F32, offset=usq0a),
            Aat("mscr1", [128, 512], F32, offset=usq1a)]
    rb_sb = A("rb_sb", [128, 512], BF16)

    ab1_sb = Aat("ab1_sb", [128, 2 * TT], BF16, offset=nc.lookup_mloc(wcat_f).addr)

    P = [nc.alloc_psum_tensor(f"P{i}", [128, TT], F32) for i in range(4)]

    QK = ["t", "v", "a", "s", "g"]
    Q = {k: [] for k in QK}
    CNT = {k: 0 for k in ALLSEMS}

    def emit(eng, body, waits=(), inc=False, tag=""):
        isem = "p" if eng == "g" else eng
        if inc:
            assert isem in ENG
            CNT[isem] += 1
        filtered = tuple((wk, wv) for (wk, wv) in waits
                         if isinstance(wv, dict) or wv > 0)
        META.append((eng, filtered, isem if inc else None, 1, tag))

        def f(E, SEMS, body=body, filtered=filtered, inc=inc, isem=isem):
            for (wk, wv) in filtered:
                v = wv["v"] if isinstance(wv, dict) else wv
                if v > 0:
                    E.wait_ge(SEMS[wk], v)
            ins = body(E)
            if inc:
                ins.then_inc(SEMS[isem], 1)
        Q[eng].append(f)
        return CNT[isem] if isem in ENG else None

    def dma(qeng, semk, dst, src, waits=()):
        CNT[semk] += 16
        filtered = tuple((wk, wv) for (wk, wv) in waits
                         if isinstance(wv, dict) or wv > 0)
        META.append((qeng, filtered, semk, 16, "dma"))

        def f(E, SEMS, dst=dst, src=src, filtered=filtered, semk=semk):
            for (wk, wv) in filtered:
                v = wv["v"] if isinstance(wv, dict) else wv
                if v > 0:
                    E.wait_ge(SEMS[wk], v)
            E.dma_start(dst, src).then_inc(SEMS[semk], 16)
        Q[qeng].append(f)
        return CNT[semk]

    def barrier(bit):
        if not (SERIAL & bit):
            return
        snap = dict(CNT)
        for e in QK:
            emit(e, lambda E: E.nop(),
                 [(k2, snap[k2]) for k2 in ALLSEMS if k2 != e], inc=(e in ENG))

    dma("s", "c", bq_sb[:], bqd[:])
    dma("s", "c", bb_sb[:], bbd[:])
    dma("s", "c", sel_sb[:], seld.rearrange("r h p -> r (h p)"))
    dma("s", "c", onesc[:], onescd[:])
    dma("s", "c", onesr[:], onesrd[:])
    dma("s", "c", epsln[:], epslnd[:])
    C_TOT = dma("s", "c", epsden[:], epsdend[:])
    dma("g", "m", m_bf[:], mhd.rearrange("p h e -> p (h e)"))
    dma("g", "m", zm_bf[:], zmd.rearrange("p h r -> p (h r)"))
    M_TOT = dma("g", "m", enc_bf[:], encd.rearrange("p k r -> p (k r)"))

    cslice = lambda t_, e: t_.rearrange("p (k e r) -> p k e r", k=8, e=3, r=40)[:, :, e, :]
    mslab3 = mkf.rearrange("p (k e r) -> p k e r", k=8, e=3, r=40)
    wtmp3 = wtmp.rearrange("p (k r) -> p k r", k=8, r=40)
    v_wg = [0] * 4
    for g in range(4):
        pg = g % 2
        w_war = [("v", v_wg[g - 1])] if g >= 1 else []
        c_cf = dma("s", "cf", cst[pg][:],
                   cfd[:, g * 8:(g + 1) * 8].rearrange("p k e r -> p (k e r)"),
                   waits=w_war)
        c_mk = dma("s", "mk", mst[pg][:],
                   mkd[:, g * 8:(g + 1) * 8].rearrange("p k e r -> p (k e r)"),
                   waits=w_war)
        wslab = wcat_f.rearrange("p (k r) -> p k r", k=KC, r=40)[:, g * 8:(g + 1) * 8, :]
        vc = emit("v", lambda E, pg=pg: E.tensor_copy(mkf[:], mst[pg][:]),
                  [("cf", c_cf), ("mk", c_mk)], inc=True)
        vc = emit("v", lambda E, pg=pg, wslab=wslab: E.tensor_tensor(
            wslab, cslice(cst[pg], 0), mslab3[:, :, 0, :], ALU.mult),
            [("v", vc)], inc=True)
        for e in (1, 2):
            vc = emit("v", lambda E, pg=pg, e=e: E.tensor_tensor(
                wtmp3, cslice(cst[pg], e), mslab3[:, :, e, :], ALU.mult),
                [("v", vc)], inc=True)
            vc = v_wg[g] = emit("v", lambda E, wslab=wslab: E.tensor_tensor(
                wslab, wslab, wtmp3, ALU.add), [("v", vc)], inc=True)
    v_wcat = emit("v", lambda E: E.tensor_copy(wcat_bf[:], wcat_f[:]),
                  [("v", v_wg[3])], inc=True)
    if DBG:
        dma("s", "db", dWc[:], wcat_bf[:], waits=[("v", v_wcat)])

    class Fold:
        def __init__(self):
            self.load_cnt = [0] * 32
            self.add_cell = [{"v": 0} for _ in range(32)]
            self.write_cell = [{"v": 0} for _ in range(32)]

        def emit_loads(self, p):
            cq, kv = p // 8, p % 8
            s = p % 2
            semk = f"fo{s}"
            cols = slice(cq * 1024, (cq + 1) * 1024)
            for gg in range(4):
                if p < 2:
                    w = [("v", v_wcat)]
                elif gg == 0:
                    w = [(f"fw{p % 2}", self.write_cell[p - 2])]
                else:
                    w = [("p", self.add_cell[p - 2])] if gg == 1 else []
                self.load_cnt[p] = dma("g", semk, fbuf[s][gg][:],
                                       wod[(kv * 4 + gg) * 128:(kv * 4 + gg + 1) * 128,
                                           cols],
                                       waits=w)

        def emit_adds_write(self, p):
            cq, kv = p // 8, p % 8
            s = p % 2
            semk = f"fo{s}"
            vv = emit("g", lambda E, s=s: E.tensor_add(
                fbuf[s][0][:], fbuf[s][0][:], fbuf[s][1][:]),
                [(semk, self.load_cnt[p])], inc=True)
            vv = emit("g", lambda E, s=s: E.tensor_add(
                fbuf[s][0][:], fbuf[s][0][:], fbuf[s][2][:]), [("p", vv)], inc=True)
            vv = emit("g", lambda E, s=s: E.tensor_add(
                fbuf[s][0][:], fbuf[s][0][:], fbuf[s][3][:]), [("p", vv)], inc=True)
            self.add_cell[p]["v"] = vv
            self.write_cell[p]["v"] = dma(
                "s", f"fw{s}",
                wofs[cq * 8:(cq + 1) * 8, :, kv, :].rearrange("m p c -> p m c"),
                fbuf[s][0].rearrange("p (m c) -> p m c", m=8, c=128),
                waits=[("p", vv)])

    fold = Fold()


    out_tt_done = [0] * KC
    outdma_done = [0] * KC
    v_done_all = 0
    q_done = {}
    f_done = {}
    fev_done = {}
    aprep_T = 0
    aprep_A = 0
    prev_sq_gate = [0] * KC
    prev_out_tt_all = 0

    for t in range(NTILES):
        tok0 = t * TT
        barrier(1024)

        xg_tot = [0] * 4
        wq_dma = {}
        for k in range(KC):
            w = [("v", out_tt_done[k])] if t > 0 else []
            if DBG and t > 0:
                w.append(("db", DBGG["done"]))
            xg_tot[k // 8] = dma("g", f"x{k // 8}", xbuf[:, k * TT:(k + 1) * TT],
                                 xT[k * 128:(k + 1) * 128, tok0:tok0 + TT], waits=w)
            if k % 4 == 3 and k // 4 < 2:
                h = k // 4
                wqw = []
                if t > 0:
                    wqw.append(("t", q_done.get((t - 1, h + HKV - 2), 0)))
                    wqw.append(("v", v_done_all))
                    if DBG:
                        wqw.append(("db", DBGG["done"]))
                wq_dma[h] = dma("g", f"wq{h % 2}", wq_sb[h % 2][:],
                                wqd[h].rearrange("p k o -> p (k o)"), waits=wqw)

        wqcell = {}
        if t == 0:
            WQPOS = {0: 2, 2: 3, 4: 4, 6: 5, 8: 6, 10: 7}
            for p in range(32):
                fold.emit_loads(p)
                if p >= 1:
                    fold.emit_adds_write(p - 1)
                if p in WQPOS:
                    h = WQPOS[p]
                    wqcell[h] = {"v": 0}
                    wq_dma[h] = dma("g", f"wq{h % 2}", wq_sb[h % 2][:],
                                    wqd[h].rearrange("p k o -> p (k o)"),
                                    waits=[("t", wqcell[h])])
            fold.emit_adds_write(31)

        dump_sq = {}
        elu_done = {}
        num_T = {}
        den_T = 0
        numev_A = {}

        def emit_q(h):
            Pq = P[h % 2]
            last = 0
            for k in range(KC):
                wts = [(f"x{k // 8}", xg_tot[k // 8])] if k % 8 == 0 else []
                if k == 0:
                    wts.append((f"wq{h % 2}", wq_dma[h]))
                    if h >= 2:
                        wts.append(("v", elu_done[(h - 2, 1)]))
                    elif t > 0:
                        wts.append(("a", fev_done[(t - 1, 30 + (h % 2))]))
                for nt in range(NT2):
                    last = emit("t", lambda E, Pq=Pq, h=h, k=k, nt=nt: E.matmul(
                        Pq[:, nt * 512:(nt + 1) * 512],
                        wq_sb[h % 2][:, k * 128:(k + 1) * 128],
                        xbuf[:, k * TT + nt * 512: k * TT + (nt + 1) * 512],
                        start=(k == 0), stop=(k == KC - 1)),
                        waits=wts if nt == 0 else (), inc=(k == KC - 1))
            q_done[(t, h)] = last

        def emit_wq_prefetch(h):
            if t == 0:
                wqcell[h]["v"] = q_done[(t, h - 2)]
            else:
                wq_dma[h] = dma("g", f"wq{h % 2}", wq_sb[h % 2][:],
                                wqd[h].rearrange("p k o -> p (k o)"),
                                waits=[("t", q_done[(t, h - 2)])])

        def emit_elu(h):
            Pq = P[h % 2]
            for nt in range(NT2):
                wts = [("t", q_done[(t, h)])]
                if h == 0 and nt == 0:
                    wts.append(("c", C_TOT))
                    wts.append(("t", prev_sq_gate[KC - 1]))
                if h >= 2 and nt == 0:
                    wts.append(("t", den_T))
                if SERIAL & 8192 and h >= 1 and nt == 0:
                    wts.append(("a", numev_A.get(h - 1, 0)))
                emit("v", lambda E, Pq=Pq, h=h, nt=nt: E.tensor_scalar(
                    mscr[nt][:], Pq[:, nt * 512:(nt + 1) * 512],
                    bq_sb[:, h:h + 1], 0.0, ALU.add, ALU.min), wts, inc=True)
                a_rb = emit("a", lambda E, Pq=Pq, h=h, nt=nt: E.activation(
                    rb_sb[:], Pq[:, nt * 512:(nt + 1) * 512], AF.Relu,
                    bias=bq_sb[:, h:h + 1]),
                    [("t", q_done[(t, h)]), ("v", CNT["v"])], inc=True)
                vaw = [("v", CNT["v"])]
                if DBG and t == 0 and h >= 2 and nt == 0:
                    vaw.append(("db", dump_sq[h - 2]))
                if DBG and t == 1 and h == 0 and nt == 0:
                    vaw.append(("db", DBGG["done"]))
                va = emit("a", lambda E, h=h, nt=nt: E.activation(
                    sq_sb[h % 2][:, nt * 512:(nt + 1) * 512], mscr[nt][:], AF.Exp),
                    vaw, inc=True)
                elu_done[(h, nt)] = emit("v", lambda E, h=h, nt=nt: E.tensor_tensor(
                    sq_sb[h % 2][:, nt * 512:(nt + 1) * 512],
                    sq_sb[h % 2][:, nt * 512:(nt + 1) * 512],
                    rb_sb[:], ALU.add), [("a", va)], inc=True)
            if DBG and t == 0:
                dump_sq[h] = dma("s", "db", dSq[h], sq_sb[h % 2][:],
                                 waits=[("v", elu_done[(h, 1)])])

        def emit_numden(h):
            nonlocal den_T
            for nt in range(NT2):
                wts = [("v", elu_done[(h, nt)])]
                if nt == 0:
                    if h > 0:
                        wts.append(("a", numev_A[h - 1]))
                    elif t > 0:
                        wts.append(("v", prev_out_tt_all))
                    if h == 0 and t == 0:
                        wts.append(("m", M_TOT))
                num_T[(h, nt)] = emit("t", lambda E, h=h, nt=nt: E.matmul(
                    P[2][:, nt * 512:(nt + 1) * 512],
                    m_bf[:, h * 128:(h + 1) * 128],
                    sq_sb[h % 2][:, nt * 512:(nt + 1) * 512],
                    start=True, stop=True), wts, inc=True)
            for nt in range(NT2):
                wts = []
                if h == 0 and nt == 0:
                    if t == 0:
                        wts.append(("m", M_TOT))
                    if t > 0:
                        wts.append(("v", prev_out_tt_all))
                den_T = emit("t", lambda E, h=h, nt=nt: E.matmul(
                    P[3][0:HKV, nt * 512:(nt + 1) * 512],
                    zm_bf[:, h * HKV:(h + 1) * HKV],
                    sq_sb[h % 2][:, nt * 512:(nt + 1) * 512],
                    start=(h == 0), stop=(h == HKV - 1)), wts, inc=True)
            for nt in range(NT2):
                nev = [("t", num_T[(h, nt)])]
                if DBG and t == 1 and h == 0 and nt == 0:
                    nev.append(("db", DBGG["done"]))
                numev_A[h] = emit("a", lambda E, h=h, nt=nt: E.activation(
                    retr[:, h * TT + nt * 512: h * TT + (nt + 1) * 512],
                    P[2][:, nt * 512:(nt + 1) * 512], AF.Copy),
                    nev, inc=True)

        if SERIAL & 8192:
            for h in range(HKV):
                emit_q(h)
                if h + 2 < HKV:
                    emit_wq_prefetch(h + 2)
                emit_elu(h)
                if t == 0:
                    for p in range(4 * h, min(4 * h + 4, 32)):
                        fold.emit_adds_write(p)
                emit_numden(h)

        else:
            emit_q(0)
            emit_wq_prefetch(2)
            emit_elu(0)
            for h in range(1, HKV):
                emit_q(h)
                if h + 2 < HKV:
                    emit_wq_prefetch(h + 2)
                emit_elu(h)
                emit_numden(h - 1)
            emit_numden(HKV - 1)

        barrier(1)

        if DBG and t == 0:
            dma("s", "db", dWofs[:], wofs[:], waits=[("fw0", 256), ("fw1", 256)])
        a_den = emit("a", lambda E: E.activation(
            den_sb[:], P[3][0:HKV, :], AF.Copy, bias=float(EPS_DEN)),
            [("t", den_T)], inc=True)
        dden_c = 0
        if DBG and t == 0:
            dden_c = dma("s", "db", dDen[:], den_sb[:], waits=[("a", a_den)])
        v_rec = emit("v", lambda E: E.reciprocal(den_sb[:], den_sb[:]),
                     [("a", a_den)] + ([("db", dden_c)] if (DBG and t == 0) else []),
                     inc=True)
        a_rden = emit("a", lambda E: E.activation(rden_bf[:], den_sb[:], AF.Copy),
                      [("v", v_rec)], inc=True)
        retr_V = 0
        for h in range(HKV):
            bc_T = {}
            for nt in range(NT2):
                wts = [("a", a_rden), ("c", C_TOT)]
                if h > 0:
                    wts.append(("v", retr_V))
                bc_T[nt] = emit("t", lambda E, h=h, nt=nt: E.matmul(
                    P[3][:, nt * 512:(nt + 1) * 512],
                    sel_sb[:, h * 128:(h + 1) * 128],
                    rden_bf[0:HKV, nt * 512:(nt + 1) * 512],
                    start=True, stop=True), wts, inc=True)
            for nt in range(NT2):
                retr_V = emit("v", lambda E, h=h, nt=nt: E.tensor_tensor(
                    retr[:, h * TT + nt * 512: h * TT + (nt + 1) * 512],
                    retr[:, h * TT + nt * 512: h * TT + (nt + 1) * 512],
                    P[3][:, nt * 512:(nt + 1) * 512], ALU.mult),
                    [("t", bc_T[nt])], inc=True)
        barrier(2)
        if DBG and t == 0:
            dma("s", "db", dRetr[:], retr[:], waits=[("v", retr_V)])

        wofm_dma = {}
        wof_T = {}
        u_done = [0] * KC
        for m in range(2):
            wts = []
            if t == 0:
                wts += [("fw0", 16 * 16), ("fw1", 16 * 16)]
            if t > 0:
                wts.append(("v", v_done_all))
                if DBG:
                    wts.append(("db", DBGG["done"]))
            wofm_dma[m] = dma("s", f"wf{m % 2}", wofm[m % 2][:],
                              wofs[m].rearrange("p kv c -> p (kv c)"), waits=wts)
        for m in range(KC):
            last = 0
            for kv in range(HKV):
                wts = []
                if kv == 0:
                    wts = [(f"wf{m % 2}", wofm_dma[m]), ("v", retr_V)]
                    if m >= 2:
                        wts.append(("v", u_done[m - 2]))
                    elif t > 0:
                        wts.append(("a", fev_done[(t - 1, 30 + (m % 2))]))
                for nt in range(NT2):
                    last = emit("t", lambda E, m=m, kv=kv, nt=nt: E.matmul(
                        P[m % 2][:, nt * 512:(nt + 1) * 512],
                        wofm[m % 2][:, kv * 128:(kv + 1) * 128],
                        retr[:, kv * TT + nt * 512: kv * TT + (nt + 1) * 512],
                        start=(kv == 0), stop=(kv == HKV - 1)),
                        waits=wts if nt == 0 else (), inc=(kv == HKV - 1))
            wof_T[m] = last
            if m + 2 < KC:
                wofm_dma[m + 2] = dma("s", f"wf{m % 2}", wofm[m % 2][:],
                                      wofs[m + 2].rearrange("p kv c -> p (kv c)"),
                                      waits=[("t", wof_T[m])])
            u_done[m] = emit("v", lambda E, m=m: E.tensor_tensor(
                xbuf[:, m * TT:(m + 1) * TT], xbuf[:, m * TT:(m + 1) * TT],
                P[m % 2][:], ALU.add), [("t", wof_T[m])], inc=True)
            if DBG and t == 0:
                dump_u[m] = dma("s", "db", dU[:, m * TT:(m + 1) * TT],
                                xbuf[:, m * TT:(m + 1) * TT],
                                waits=[("v", u_done[m])])
        barrier(4)

        if t == 0:
            apmm_k = [0] * KC
            G_dma = [0] * KC
            G_dma[0] = dma("g", "ga", wb_sb[0][:], gtd[0:128, :])
            G_dma[1] = dma("g", "gb", wb_sb[1][:], gtd[128:256, :])
            for k in range(KC):
                wts = [(("ga" if k % 2 == 0 else "gb"), G_dma[k])]
                if k == 0:
                    wts += [("v", max(u_done)), ("v", v_wcat)]
                for jp in range(8):
                    apmm_k[k] = emit("t", lambda E, k=k, jp=jp: E.matmul(
                        P[jp // 2][0:40, (jp % 2) * 512:(jp % 2 + 1) * 512],
                        wcat_bf[:, k * 40:(k + 1) * 40],
                        wb_sb[k % 2][:, jp * 512:(jp + 1) * 512],
                        start=(k == 0), stop=(k == KC - 1)),
                        waits=wts if jp == 0 else (), inc=True)
                if k + 2 < KC:
                    G_dma[k + 2] = dma("g", ("ga" if k % 2 == 0 else "gb"),
                                       wb_sb[k % 2][:],
                                       gtd[(k + 2) * 128:(k + 3) * 128, :],
                                       waits=[("t", apmm_k[k])])
            aprep_T = apmm_k[KC - 1]
            for jp in range(8):
                aprep_A = emit("a", lambda E, jp=jp: E.activation(
                    at_half[:, jp * 512:(jp + 1) * 512],
                    P[jp // 2][0:40, (jp % 2) * 512:(jp % 2 + 1) * 512],
                    AF.Copy, scale=0.5), [("t", aprep_T)], inc=True)
            if DBG:
                dma("s", "db", dAt[:], at_half[:], waits=[("a", aprep_A)])
        barrier(8)

        elu_last_v = elu_done[(HKV - 1, 1)]

        def layernorm(src_done, first_gate_A, usq_prev_v=0):
            mean_T = [0, 0]
            sq_T = [0, 0]
            sqmm_k = [0] * KC
            usq_A = [0] * KC
            for k in range(KC):
                wts_u = [("v", src_done[k])]
                if k == 0:
                    wts_u.append(("c", C_TOT))
                    wts_u.append(("a", first_gate_A))
                for nt in range(NT2):
                    mean_T[nt] = emit("t", lambda E, k=k, nt=nt: E.matmul(
                        P[2][0:1, nt * 512:(nt + 1) * 512], onesc[:],
                        xbuf[:, k * TT + nt * 512: k * TT + (nt + 1) * 512],
                        start=(k == 0), stop=(k == KC - 1)),
                        waits=wts_u if nt == 0 else (), inc=(k == KC - 1))
                wsq = [("v", src_done[k]),
                       ("t", prev_sq_gate[KC - 2 + k] if k < 2 else sqmm_k[k - 2])]
                if k < 2 and usq_prev_v:
                    wsq.append(("v", usq_prev_v))
                usq_A[k] = emit("a", lambda E, k=k: E.activation(
                    usq[k % 2][:], xbuf[:, k * TT:(k + 1) * TT], AF.Square),
                    wsq, inc=True)
                for nt in range(NT2):
                    sq_T[nt] = emit("t", lambda E, k=k, nt=nt: E.matmul(
                        P[3][0:1, nt * 512:(nt + 1) * 512], onesc[:],
                        usq[k % 2][:, nt * 512:(nt + 1) * 512],
                        start=(k == 0), stop=(k == KC - 1)),
                        waits=[("a", usq_A[k])] if nt == 0 else (),
                        inc=(nt == NT2 - 1 or k == KC - 1))
                sqmm_k[k] = sq_T[NT2 - 1]
            R = lambda i: rows[0:1, i * 512:(i + 1) * 512]
            a_mean = [0, 0]
            for nt in range(NT2):
                a_mean[nt] = emit("a", lambda E, nt=nt: E.activation(
                    R(0 + nt), P[2][0:1, nt * 512:(nt + 1) * 512], AF.Copy,
                    scale=1.0 / D), [("t", mean_T[nt])], inc=True)
                emit("a", lambda E, nt=nt: E.activation(
                    R(2 + nt), P[3][0:1, nt * 512:(nt + 1) * 512], AF.Copy,
                    scale=1.0 / D), [("t", sq_T[nt])], inc=True)
            a_msq = CNT["a"]
            v_mm = emit("v", lambda E: (
                E.tensor_mul(R(4), R(0), R(0)),
                E.tensor_mul(R(5), R(1), R(1)))[-1], [("a", a_msq)], inc=True)
            v_var = emit("v", lambda E: (
                E.tensor_sub(R(2), R(2), R(4)),
                E.tensor_sub(R(3), R(3), R(5)))[-1], [("v", v_mm)], inc=True)
            a_ln = emit("a", lambda E: (
                E.activation(R(4), R(2), AF.Ln, bias=epsln[:]),
                E.activation(R(5), R(3), AF.Ln, bias=epsln[:]))[-1],
                [("v", v_var), ("c", C_TOT)], inc=True)
            a_rstd = emit("a", lambda E: (
                E.activation(R(6), R(4), AF.Exp, scale=-0.5),
                E.activation(R(7), R(5), AF.Exp, scale=-0.5))[-1],
                [("a", a_ln)], inc=True)
            v_brow = emit("v", lambda E: (
                E.tensor_mul(R(4), R(0), R(6)),
                E.tensor_mul(R(5), R(1), R(7)))[-1], [("a", a_rstd)], inc=True)
            Rb = lambda i: rowsb[0:1, i * 512:(i + 1) * 512]
            a_rowsb = emit("a", lambda E: (
                E.activation(Rb(0), R(6), AF.Copy),
                E.activation(Rb(1), R(7), AF.Copy),
                E.activation(Rb(2), R(4), AF.Copy),
                E.activation(Rb(3), R(5), AF.Copy))[-1], [("v", v_brow)], inc=True)
            return a_rowsb, sqmm_k, max(a_mean)

        ln1_rowsb, ln1_sqmm, ln1_meanA = layernorm(
            u_done, aprep_A if t == 0 else 0, usq_prev_v=elu_last_v)
        prev_sq_gate = ln1_sqmm
        bc1_T = 0
        for nt in range(NT2):
            bc1_T = emit("t", lambda E, nt=nt: E.matmul(
                P[0][:, nt * 512:(nt + 1) * 512], onesr[:],
                rowsb[0:1, nt * 512:(nt + 1) * 512], start=True, stop=True),
                [("a", ln1_rowsb), ("c", C_TOT)] if nt == 0 else (), inc=True)
            bc1_T = emit("t", lambda E, nt=nt: E.matmul(
                P[1][:, nt * 512:(nt + 1) * 512], onesr[:],
                rowsb[0:1, (2 + nt) * 512:(3 + nt) * 512], start=True, stop=True),
                (), inc=True)
        a_ab1a = emit("a", lambda E: E.activation(ab1_sb[:, 0:TT], P[0][:], AF.Copy),
                      [("t", bc1_T)], inc=True)
        a_ab1b = emit("a", lambda E: E.activation(ab1_sb[:, TT:2 * TT], P[1][:], AF.Copy),
                      [("t", bc1_T)], inc=True)
        norm_done = [0] * KC
        for k in range(KC):
            nw = [("a", a_ab1b)] if k == 0 else []
            if DBG and t == 0:
                nw.append(("db", dump_u[k]))
            v_mul = emit("v", lambda E, k=k: E.tensor_tensor(
                xbuf[:, k * TT:(k + 1) * TT], xbuf[:, k * TT:(k + 1) * TT],
                ab1_sb[:, 0:TT], ALU.mult), nw, inc=True)
            norm_done[k] = emit("v", lambda E, k=k: E.tensor_tensor(
                xbuf[:, k * TT:(k + 1) * TT], xbuf[:, k * TT:(k + 1) * TT],
                ab1_sb[:, TT:2 * TT], ALU.subtract), [("v", v_mul)], inc=True)
            if DBG and t == 0:
                dump_y[k] = dma("s", "db", dY[:, k * TT:(k + 1) * TT],
                                xbuf[:, k * TT:(k + 1) * TT],
                                waits=[("v", norm_done[k])])
        barrier(16)

        wb_dma = {}

        def emit_wb(m, wts):
            semk = f"wb{m % 2}"
            flat = wbd[m].rearrange("p k o -> p (k o)")
            h0 = dma("g", semk, wb_sb[m % 2][:, 0:2048], flat[:, 0:2048], waits=wts)
            h1 = dma("g", semk, wb_sb[m % 2][:, 2048:4096], flat[:, 2048:4096])
            wb_dma[m] = (h0, h1)

        for m in range(2):
            wts = [("t", aprep_T)] if t == 0 else \
                [("t", f_done.get((t - 1, KC - 2 + m), 0))]
            emit_wb(m, wts)
        e_T = 0
        for k in range(KC):
            wts = [("v", norm_done[k])]
            if k == 0:
                wts += [("m", M_TOT), ("a", ln1_meanA)]
            for nt in range(NT2):
                e_T = emit("t", lambda E, k=k, nt=nt: E.matmul(
                    P[2][0:40, nt * 512:(nt + 1) * 512],
                    enc_bf[:, k * 40:(k + 1) * 40],
                    xbuf[:, k * TT + nt * 512: k * TT + (nt + 1) * 512],
                    start=(k == 0), stop=(k == KC - 1)),
                    waits=wts if nt == 0 else (), inc=(k == KC - 1))
            for m in range(2):
                kw = []
                if k == 0:
                    kw.append((f"wb{m % 2}", wb_dma[m][0]))
                if k == 16:
                    kw.append((f"wb{m % 2}", wb_dma[m][1]))
                for nt in range(NT2):
                    emit("t", lambda E, m=m, k=k, nt=nt: E.matmul(
                        P[m % 2][:, nt * 512:(nt + 1) * 512],
                        wb_sb[m % 2][:, k * 128:(k + 1) * 128],
                        xbuf[:, k * TT + nt * 512: k * TT + (nt + 1) * 512],
                        start=(k == 0), stop=False),
                        waits=kw if nt == 0 else ())
        ebf_A = emit("a", lambda E: E.activation(e_bf[:], P[2][0:40, :], AF.Copy),
                     [("t", e_T)], inc=True)
        if DBG and t == 0:
            dma("s", "db", dEbf[:], e_bf[:], waits=[("a", ebf_A)])
        barrier(32)
        for m in range(2):
            last = 0
            for nt in range(NT2):
                wts = [("a", ebf_A)] if (m == 0 and nt == 0) else ()
                last = emit("t", lambda E, m=m, nt=nt: E.matmul(
                    P[m % 2][:, nt * 512:(nt + 1) * 512],
                    at_half[:, m * 128:(m + 1) * 128],
                    e_bf[0:40, nt * 512:(nt + 1) * 512],
                    start=False, stop=True), waits=wts, inc=True)
            f_done[(t, m)] = last
            fev_done[(t, m)] = emit("a", lambda E, m=m: E.activation(
                freg[:, m * TT:(m + 1) * TT], P[m % 2][:], AF.Copy),
                [("t", f_done[(t, m)])], inc=True)
            wbw = [("t", f_done[(t, m)])]
            if SERIAL & 4096:
                wbw.append(("a", fev_done[(t, m)]))
            emit_wb(m + 2, wbw)
        for m in range(2, KC):
            Pf = P[m % 2]
            last = 0
            for nt in range(NT2):
                wts = ()
                if nt == 0:
                    wts = [("a", ebf_A), (f"wb{m % 2}", wb_dma[m][0]),
                           ("a", fev_done[(t, m - 2)])]
                    if SERIAL & 4096:
                        wts.append(("a", fev_done[(t, m - 1)]))
                emit("t", lambda E, Pf=Pf, m=m, nt=nt: E.matmul(
                    Pf[:, nt * 512:(nt + 1) * 512],
                    at_half[:, m * 128:(m + 1) * 128],
                    e_bf[0:40, nt * 512:(nt + 1) * 512],
                    start=True, stop=False), waits=wts)
                for k in range(KC):
                    kw = [(f"wb{m % 2}", wb_dma[m][1])] if (nt == 0 and k == 16) else ()
                    last = emit("t", lambda E, Pf=Pf, m=m, k=k, nt=nt: E.matmul(
                        Pf[:, nt * 512:(nt + 1) * 512],
                        wb_sb[m % 2][:, k * 128:(k + 1) * 128],
                        xbuf[:, k * TT + nt * 512: k * TT + (nt + 1) * 512],
                        start=False, stop=(k == KC - 1)),
                        waits=kw, inc=(k == KC - 1))
            f_done[(t, m)] = last
            fev_done[(t, m)] = emit("a", lambda E, m=m: E.activation(
                freg[:, m * TT:(m + 1) * TT], P[m % 2][:], AF.Copy),
                [("t", f_done[(t, m)])], inc=True)
            if m + 2 < KC:
                wbw = [("t", f_done[(t, m)])]
                if SERIAL & 4096:
                    wbw.append(("a", fev_done[(t, m)]))
                emit_wb(m + 2, wbw)
        barrier(64)
        vv_done = [0] * KC
        for m in range(KC):
            vw = [("a", fev_done[(t, m)]), ("t", f_done[(t, KC - 1)])]
            if m == 0:
                vw.append(("c", C_TOT))
            if DBG and t == 0:
                vw.append(("db", dump_y[m]))
                dma("s", "db", dF[:, m * TT:(m + 1) * TT], freg[:, m * TT:(m + 1) * TT],
                    waits=[("a", fev_done[(t, m)])])
            vv_done[m] = emit("v", lambda E, m=m: E.scalar_tensor_tensor(
                xbuf[:, m * TT:(m + 1) * TT], freg[:, m * TT:(m + 1) * TT],
                bb_sb[:, m:m + 1], xbuf[:, m * TT:(m + 1) * TT],
                ALU.add, ALU.add), vw, inc=True)
            if DBG and t == 0:
                dma("s", "db", dV[:, m * TT:(m + 1) * TT], xbuf[:, m * TT:(m + 1) * TT],
                    waits=[("v", vv_done[m])])
        v_done_all = vv_done[KC - 1]
        barrier(128)

        ln2_rowsb, ln2_sqmm, _ = layernorm(vv_done, ebf_A)
        prev_sq_gate = ln2_sqmm
        barrier(256)
        bc2_T = 0
        for nt in range(NT2):
            bc2_T = emit("t", lambda E, nt=nt: E.matmul(
                P[2][:, nt * 512:(nt + 1) * 512], onesr[:],
                rowsb[0:1, nt * 512:(nt + 1) * 512], start=True, stop=True),
                [("a", ln2_rowsb)] if nt == 0 else (), inc=True)
            bc2_T = emit("t", lambda E, nt=nt: E.matmul(
                P[3][:, nt * 512:(nt + 1) * 512], onesr[:],
                rowsb[0:1, (2 + nt) * 512:(3 + nt) * 512], start=True, stop=True),
                (), inc=True)
        new_outdma = [0] * KC
        for k in range(KC):
            wts = [("t", bc2_T)] if k == 0 else []
            ping_gate = CNT["out"] if (k >= 2 or t > 0) else 0
            if ping_gate:
                wts.append(("out", ping_gate))
            v_mul2 = emit("v", lambda E, k=k: E.tensor_tensor(
                outf[k % 2][:], xbuf[:, k * TT:(k + 1) * TT], P[2][:], ALU.mult),
                wts, inc=True)
            out_tt_done[k] = emit("v", lambda E, k=k: E.tensor_tensor(
                outf[k % 2][:], outf[k % 2][:], P[3][:], ALU.subtract),
                [("v", v_mul2)], inc=True)
            new_outdma[k] = dma("s", "out", oT[k * 128:(k + 1) * 128, tok0:tok0 + TT],
                                outf[k % 2][:], waits=[("v", out_tt_done[k])])
        barrier(512)
        outdma_done = new_outdma
        prev_out_tt_all = out_tt_done[KC - 1]
        if DBG and t == 0:
            DBGG["done"] = CNT["db"]

    emit("s", lambda E: E.nop(), [(k, CNT[k]) for k in DMAS if CNT[k] > 0])

    with ExitStack() as ctx:
        sems = {k: ctx.enter_context(nc.semaphore(f"S_{k}")) for k in ALLSEMS}
        block = ctx.enter_context(nc.Block(no_gpsimd_drain=True))
        block.tensor(lambda E: [f(E, sems) for f in Q["t"]])
        block.vector(lambda E: [f(E, sems) for f in Q["v"]])
        block.scalar(lambda E: [f(E, sems) for f in Q["a"]])
        block.sync(lambda E: [f(E, sems) for f in Q["s"]])
        block.gpsimd(lambda E: [f(E, sems) for f in Q["g"]])
    return nc


def _prep_inputs(inputs):
    _ = inputs.get("L0_attn_Wk"), inputs.get("L0_attn_bk")
    _ = inputs.get("L0_attn_Wv"), inputs.get("L0_attn_bv")
    f32 = np.float32
    x = np.ascontiguousarray(np.asarray(inputs["L0_attn_hidden_states"], f32).reshape(16384, D))
    Wq = np.asarray(inputs["L0_attn_Wq"], f32)
    bq = np.asarray(inputs["L0_attn_bq"], f32)
    Wo = np.asarray(inputs["L0_attn_Wo"], f32)
    M = np.asarray(inputs["L0_attn_M"], f32)
    z = np.asarray(inputs["L0_attn_z"], f32)
    Wb = np.asarray(inputs["L0_ffn_W_base"], f32)
    bb = np.asarray(inputs["L0_ffn_b_base"], f32)
    e0 = np.asarray(inputs["L0_ffn_enc0"], f32)
    e1 = np.asarray(inputs["L0_ffn_enc1"], f32)
    c0 = np.asarray(inputs["L0_ffn_coefs0"], f32)
    c1 = np.asarray(inputs["L0_ffn_coefs1"], f32)
    m0 = np.asarray(inputs["L0_ffn_mask0"])
    m1 = np.asarray(inputs["L0_ffn_mask1"])

    Wq_sel = Wq.reshape(32, 128, D)[::4].reshape(1024, D)
    wq_blk = np.ascontiguousarray(Wq_sel.T.reshape(KC, 128, HKV, 128).transpose(2, 1, 0, 3))
    bq_blk = np.ascontiguousarray(bq.reshape(32, 128)[::4].T)
    mh_blk = np.ascontiguousarray(M.transpose(1, 0, 2))
    zm_blk = np.zeros((128, HKV, HKV), f32)
    for h in range(HKV):
        zm_blk[:, h, h] = z[h]
    woT = np.ascontiguousarray(Wo.T)
    wb_blk = np.ascontiguousarray(Wb.T.reshape(KC, 128, KC, 128).transpose(2, 1, 0, 3))
    bb_blk = np.ascontiguousarray(bb.reshape(KC, 128).T)
    enc_cat = np.concatenate([e0, e1], axis=0)
    enc_blk = np.ascontiguousarray(enc_cat.T.reshape(KC, 128, 40).transpose(1, 0, 2))
    ccat = np.concatenate([c0.transpose(1, 0, 2), c1.transpose(1, 0, 2)], axis=2)
    cf_blk = np.ascontiguousarray(ccat.reshape(KC, 128, 3, 40).transpose(1, 0, 2, 3))
    mcat = np.concatenate([m0.transpose(1, 0, 2), m1.transpose(1, 0, 2)], axis=2)
    mk_blk = np.ascontiguousarray(mcat.reshape(KC, 128, 3, 40).transpose(1, 0, 2, 3)).view(np.uint8)

    sel_np = np.zeros((HKV, HKV, 128), f32)
    for h in range(HKV):
        sel_np[h, h, :] = 1.0
    shared = {
        "wqd": wq_blk, "bqd": bq_blk, "mhd": mh_blk, "zmd": zm_blk, "wod": woT,
        "wbd": wb_blk, "bbd": bb_blk, "encd": enc_blk, "cfd": cf_blk, "mkd": mk_blk,
        "gtd": np.ascontiguousarray(_build_G()),
        "seld": sel_np.astype(BF),
        "onescd": np.ones((128, 1), f32).astype(BF),
        "onesrd": np.ones((1, 128), f32).astype(BF),
        "epslnd": np.full((1, 1), EPS_LN, f32),
        "epsdend": np.full((HKV, 1), EPS_DEN, f32),
    }
    in_maps = []
    for c in range(NCORES):
        m = dict(shared)
        m["xT"] = np.ascontiguousarray(x[c * NTOK:(c + 1) * NTOK].T)
        in_maps.append(m)
    return in_maps


def kernel(**inputs):
    if "nc" not in _cache:
        _cache["nc"] = build_program()
    nc = _cache["nc"]
    in_maps = _prep_inputs(inputs)
    res = run_bass_kernel_spmd(nc, in_maps, core_ids=list(range(NCORES)))
    parts = [np.asarray(res.results[c]["oT"]).T for c in range(NCORES)]
    return np.concatenate(parts, axis=0).reshape(2, 8192, D).astype(np.float32)
